# version 04
# baseline (speedup 1.0000x reference)
import numpy as np
from contextlib import ExitStack

import concourse.bass as bass
import concourse.mybir as mybir
from concourse.bass_utils import run_bass_kernel_spmd

F32 = mybir.dt.float32
BF16 = mybir.dt.bfloat16
I32 = mybir.dt.int32
U8 = mybir.dt.uint8
AF = mybir.ActivationFunctionType
ALU = mybir.AluOpType

S = 8192
D = 4096
NC = 8
T = S // NC
HJ = 4
HID = 11008
HSL = HID // NC
NHT = 11
NSLOT = NC * NHT
W2D = D // NC
CSCALE = float(192.0 ** -0.5)
EXP_BIAS = -16.0
EPS = 1e-5
INV_D = 1.0 / D

ENGINES = ("sync", "gpsimd", "vector", "scalar", "tensor")


def _hu(ht):
    return 96 if ht == NHT - 1 else 128


class Sem:
    def __init__(self, ctx, nc, name):
        self.h = ctx.enter_context(nc.semaphore(name))
        self.n = 0


class Arena:
    def __init__(self, tensor, nbytes):
        self.t = tensor
        self.nbytes = nbytes

    def view(self, off, shape, dtype):
        esz = {F32: 4, BF16: 2, I32: 4}[dtype]
        n = 1
        for s in shape[1:]:
            n *= s
        assert off % 32 == 0 and off + n * esz <= self.nbytes, (off, n * esz)
        ap = self.t[0:shape[0], off:off + n * esz].bitcast(dtype)
        if len(shape) == 2:
            return ap
        pat_in = "p (" + " ".join(f"a{i}" for i in range(1, len(shape))) + ")"
        pat_out = "p " + " ".join(f"a{i}" for i in range(1, len(shape)))
        kw = {f"a{i}": shape[i] for i in range(1, len(shape))}
        return ap.rearrange(f"{pat_in} -> {pat_out}", **kw)


class E:
    def __init__(self, name, obj):
        self.name = name
        self.obj = obj
        self.rank = None

    def emit(self, eng, fn, sem=None, inc=1):
        if eng == self.name:
            inst = fn(self.obj)
            if sem is not None:
                inst.then_inc(sem.h, inc)
        if sem is not None:
            sem.n += inc
            return sem.n
        return None

    def wait(self, eng, sem, val):
        if val > 0 and eng == self.name:
            self.obj.wait_ge(sem.h, val)


def build_nc(upto=9):
    nc = bass.Bass("TRN2", target_bir_lowering=False)

    x_fm = nc.dram_tensor("x_fm", [D, T], F32, kind="ExternalInput")
    wq_sl = nc.dram_tensor("wq_sl", [D, 512], F32, kind="ExternalInput")
    wk_sl = nc.dram_tensor("wk_sl", [D, 512], F32, kind="ExternalInput")
    wv_sl = nc.dram_tensor("wv_sl", [D, 512], F32, kind="ExternalInput")
    wo_p = nc.dram_tensor("wo_p", [D, D], F32, kind="ExternalInput")
    w1_sl = nc.dram_tensor("w1_sl", [32 * HSL, 64], I32, kind="ExternalInput")
    s1_sl = nc.dram_tensor("s1_sl", [64 * HSL], F32, kind="ExternalInput")
    w3_sl = nc.dram_tensor("w3_sl", [32 * HSL, 64], I32, kind="ExternalInput")
    s3_sl = nc.dram_tensor("s3_sl", [64 * HSL], F32, kind="ExternalInput")
    w2_sl = nc.dram_tensor("w2_sl", [86 * W2D, 64], I32, kind="ExternalInput")
    s2_sl = nc.dram_tensor("s2_sl", [172 * W2D], F32, kind="ExternalInput")
    out_fm = nc.dram_tensor("out_fm", [D, T], F32, kind="ExternalOutput")

    xln_c = nc.dram_tensor("xln_c", [128, 32 * T], BF16)
    xln_g = nc.dram_tensor("xln_g", [128 * NC, 32 * T], BF16)
    attn_c = nc.dram_tensor("attn_c", [512, S], BF16)
    attn_g = nc.dram_tensor("attn_g", [512 * NC, S], BF16)
    w1t_c = nc.dram_tensor("w1t_c", [128, NHT * D], BF16)
    w1t_g = nc.dram_tensor("w1t_g", [128 * NC, NHT * D], BF16)
    w3t_c = nc.dram_tensor("w3t_c", [128, NHT * D], BF16)
    w3t_g = nc.dram_tensor("w3t_g", [128 * NC, NHT * D], BF16)
    w2t_c = nc.dram_tensor("w2t_c", [128, NSLOT * W2D], BF16)
    w2t_g = nc.dram_tensor("w2t_g", [128 * NC, NSLOT * W2D], BF16)
    q_dram = nc.dram_tensor("q_dram", [512, S], BF16)
    k_dram = nc.dram_tensor("k_dram", [512, S], BF16)
    v_dram = nc.dram_tensor("v_dram", [S, 512], BF16)
    wo_bf = nc.dram_tensor("wo_bf", [D, D], BF16)
    x1_dram = nc.dram_tensor("x1_dram", [D, T], F32)
    xl2_dram = nc.dram_tensor("xl2_dram", [D, T], BF16)

    ctx = ExitStack()
    with ctx:
        mask_big = ctx.enter_context(nc.sbuf_tensor("mask_big", [128, 896], F32))
        ident = ctx.enter_context(nc.sbuf_tensor("ident", [128, 128], BF16))
        ones_col = ctx.enter_context(nc.sbuf_tensor("ones_col", [128, 1], BF16))
        ones_row = ctx.enter_context(nc.sbuf_tensor("ones_row", [1, 128], F32))
        sc_c = ctx.enter_context(nc.sbuf_tensor("sc_c", [128, 1], F32))
        sb_b = ctx.enter_context(nc.sbuf_tensor("sb_b", [128, 1], F32))
        eps_ap = ctx.enter_context(nc.sbuf_tensor("eps_ap", [1, 1], F32))
        rows = ctx.enter_context(nc.sbuf_tensor("rows", [1, 9 * 512], F32))

        NB = (nc.sbuf_bytes_remaining - 64) // 32 * 32
        arena = Arena(ctx.enter_context(nc.sbuf_tensor("arena", [128, NB], U8)), NB)

        pb = [ctx.enter_context(nc.psum_tensor(f"pb{i}", [128, 512], F32))
              for i in range(8)]

        sems = {}
        semnames = ["sLD", "sGD", "sPE", "sVE", "sSE", "sGE", "sCC",
                    "sXF0", "sXF1", "sXLO0", "sXLO1", "sDQ",
                    "sXL0", "sXL1", "sEV0", "sEV1", "sEV2", "sEV3",
                    "sKV", "sQ0", "sQ1", "sAT0", "sAT1",
                    "sAR", "sWO0", "sWO1", "sXT0", "sXT1", "sXO0", "sXO1",
                    "sX2", "sW13_0", "sW13_1", "sW13_2", "sW13_3",
                    "sW2_0", "sW2_1", "sX1T0", "sX1T1", "sST0", "sST1", "sX1S"]
        for name in semnames:
            sems[name] = Sem(ctx, nc, name)

        blk = ctx.enter_context(nc.Block())

        off = [0]

        def al(nbytes, align=32):
            o = (off[0] + align - 1) // align * align
            off[0] = o + nbytes
            assert off[0] <= NB, (off[0], NB)
            return o

        o_xf = [al(4096), al(4096)]
        o_xb = [al(2048), al(2048)]
        o_xsq = al(2048)
        o_xlo = [al(1024), al(1024)]
        o_tmp = al(2048)
        o_pk = al(11008)
        o_lsb = al(11008)
        o_scl = al(704)
        o_dqf = al(22016)
        o_dqb = al(11008)
        o_stg = al(11264)
        o_wq = al(32768)
        o_wk = al(32768)
        o_wv = al(32768)

        off[0] = 0
        o_xl = [al(32768), al(32768)]
        o_ev = al(4 * 1024)
        assert off[0] <= o_wq

        off[0] = 0
        o_K = al(65536)
        o_V = al(65536)
        o_Q = [al(1024), al(1024)]
        o_pr = [al(1024), al(1024)]
        o_li = al(2048)
        o_ao = [al(1024), al(1024)]

        off[0] = 0
        o_ar = al(32768)
        o_wo = [al(16384), al(16384)]
        o_x1s = al(65536)
        o_xt = [al(2048), al(2048)]
        o_b1 = al(1024)
        o_b2 = al(1024)
        o_t2 = al(2048)
        o_xl2o = [al(1024), al(1024)]

        off[0] = 0
        o_hact = al(90112)
        o_xl2 = al(32768)
        o_w13 = [al(8192), al(8192), al(8192), al(8192)]
        o_ht = al(2048)
        ffn2_base = o_xl2
        off[0] = ffn2_base
        o_w2b = [al(22528), al(22528)]
        o_x1t = [al(2048), al(2048)]
        o_os = [al(2048), al(2048)]

        def program(E_):
            _dummy_done = [False]
            for s_ in sems.values():
                s_.n = 0
            em, wt = E_.emit, E_.wait
            sLD, sGD, sPE, sVE, sSE, sGE, sCC = (
                sems["sLD"], sems["sGD"], sems["sPE"], sems["sVE"], sems["sSE"],
                sems["sGE"], sems["sCC"])
            sXF = [sems["sXF0"], sems["sXF1"]]
            sXLO = [sems["sXLO0"], sems["sXLO1"]]
            sDQ = sems["sDQ"]
            sXL = [sems["sXL0"], sems["sXL1"]]
            sEV = [sems[f"sEV{i}"] for i in range(4)]
            sKV = sems["sKV"]
            sQ = [sems["sQ0"], sems["sQ1"]]
            sATs = [sems["sAT0"], sems["sAT1"]]
            sAR = sems["sAR"]
            sWO = [sems["sWO0"], sems["sWO1"]]
            sXT = [sems["sXT0"], sems["sXT1"]]
            sXO = [sems["sXO0"], sems["sXO1"]]
            sX2 = sems["sX2"]
            sW13 = [sems[f"sW13_{i}"] for i in range(4)]
            sW2 = [sems["sW2_0"], sems["sW2_1"]]
            sX1T = [sems["sX1T0"], sems["sX1T1"]]
            sSTs = [sems["sST0"], sems["sST1"]]
            sX1S = sems["sX1S"]

            def gdma(fn, extra_wait=None):
                wt("gpsimd", sGD, sGD.n)
                if extra_wait is not None:
                    wt("gpsimd", extra_wait[0], extra_wait[1])
                return em("gpsimd", fn, sGD, 16)

            if E_.name == "sync":
                E_.rank = nc.partition_id()

            def fence_all():
                snap = [(s_, s_.n) for s_ in sems.values()]
                for eng in ENGINES:
                    for sem, val in snap:
                        wt(eng, sem, val)

            def dummy_out():
                _dummy_done[0] = True
                sSTs = [sems["sST0"], sems["sST1"]]
                for i in range(64):
                    dt, qh = i // 2, i % 2
                    em("sync", lambda sy, dt=dt, qh=qh: sy.dma_start(
                        out=out_fm[dt * 128:(dt + 1) * 128, qh * 512:(qh + 1) * 512],
                        in_=mask_big[:, 0:512]), sSTs[i % 2], 16)
                wt("sync", sSTs[0], sSTs[0].n)
                wt("sync", sSTs[1], sSTs[1].n)
                wt("gpsimd", sSTs[0], sSTs[0].n)
                wt("gpsimd", sSTs[1], sSTs[1].n)

            em("gpsimd", lambda g: g.memset(mask_big[:, :], 0.0), sGE)
            wt("gpsimd", sGE, sGE.n)
            em("gpsimd", lambda g: g.affine_select(
                mask_big[:, :], mask_big[:, :], [[1, 896]], ALU.is_ge, -1.0e30,
                base=-384, channel_multiplier=-1), sGE)
            em("gpsimd", lambda g: g.memset(ident[:, :], 1.0), sGE)
            wt("gpsimd", sGE, sGE.n)
            em("gpsimd", lambda g: g.affine_select(
                ident[:, :], ident[:, :], [[1, 128]], ALU.is_equal, 0.0,
                base=0, channel_multiplier=-1), sGE)
            em("gpsimd", lambda g: g.memset(ones_col[:, :], 1.0), sGE)
            em("gpsimd", lambda g: g.memset(ones_row[:, :], 1.0), sGE)
            em("gpsimd", lambda g: g.memset(sc_c[:, :], CSCALE), sGE)
            em("gpsimd", lambda g: g.memset(sb_b[:, :], EXP_BIAS), sGE)
            em("gpsimd", lambda g: g.memset(eps_ap[:, :], EPS), sGE)
            stg_all = arena.view(o_stg, [128, 5632], BF16)
            em("gpsimd", lambda g: g.memset(stg_all[:, :], 0.0), sGE)
            CONST_GE = sGE.n

            wq_v = arena.view(o_wq, [128, 32, 512], BF16)
            wk_v = arena.view(o_wk, [128, 32, 512], BF16)
            wv_v = arena.view(o_wv, [128, 32, 512], BF16)
            gdma(lambda g: g.dma_start(
                out=wq_v[:, :, :],
                in_=wq_sl[:, :].rearrange("(c p) e -> p c e", p=128)))
            gdma(lambda g: g.dma_start(
                out=wk_v[:, :, :],
                in_=wk_sl[:, :].rearrange("(c p) e -> p c e", p=128)))
            gdma(lambda g: g.dma_start(
                out=wv_v[:, :, :],
                in_=wv_sl[:, :].rearrange("(c p) e -> p c e", p=128)))
            W_CAST = sGD.n
            gdma(lambda g: g.dma_start(out=wo_bf[:, :], in_=wo_p[:, :]))
            WO_CAST = sGD.n

            xb = [arena.view(o, [128, 1024], BF16) for o in o_xb]
            xsq = arena.view(o_xsq, [128, 1024], BF16)
            c_pe = [0] * 32
            for c in range(32):
                b = c % 2
                v = gdma((lambda g, c=c, b=b: g.dma_start(
                    out=xb[b][:, :], in_=x_fm[c * 128:(c + 1) * 128, :])),
                    extra_wait=(sPE, c_pe[c - 2]) if c >= 2 else None)
                wt("scalar", sGD, v)
                if c >= 1:
                    wt("scalar", sPE, c_pe[c - 1])
                sq = em("scalar", lambda s, b=b: s.activation(
                    xsq[:, :], xb[b][:, :], AF.Square), sSE)
                wt("tensor", sSE, sq)
                if c == 0:
                    wt("tensor", sGE, CONST_GE)
                for h in range(2):
                    em("tensor", lambda t, b=b, h=h, c=c: t.matmul(
                        pb[2 * h][0:1, :], ones_col[:, :],
                        xb[b][:, h * 512:(h + 1) * 512],
                        start=(c == 0), stop=(c == 31)), None)
                    last = (h == 1)
                    em("tensor", lambda t, h=h, c=c: t.matmul(
                        pb[2 * h + 1][0:1, :], ones_col[:, :],
                        xsq[:, h * 512:(h + 1) * 512],
                        start=(c == 0), stop=(c == 31)), sPE if last else None)
                c_pe[c] = sPE.n
            LN1_STATS_PE = sPE.n

            m_ = [rows[0:1, 0:512], rows[0:1, 512:1024]]
            ex2_ = [rows[0:1, 1024:1536], rows[0:1, 1536:2048]]
            var_ = [rows[0:1, 2048:2560], rows[0:1, 2560:3072]]
            rst_ = [rows[0:1, 3072:3584], rows[0:1, 3584:4096]]
            mr_ = ex2_
            wt("vector", sPE, LN1_STATS_PE)
            for h in range(2):
                em("vector", lambda v, h=h: v.tensor_scalar_mul(
                    m_[h], pb[2 * h][0:1, :], INV_D), sVE)
                v2 = em("vector", lambda v, h=h: v.tensor_scalar_mul(
                    ex2_[h], pb[2 * h + 1][0:1, :], INV_D), sVE)
                wt("vector", sVE, v2)
                v3 = em("vector", lambda v, h=h: v.tensor_mul(var_[h], m_[h], m_[h]), sVE)
                wt("vector", sVE, v3)
                v4 = em("vector", lambda v, h=h: v.tensor_sub(var_[h], ex2_[h], var_[h]), sVE)
                wt("scalar", sVE, v4)
                s1_ = em("scalar", lambda s, h=h: s.activation(
                    rst_[h], var_[h], AF.Sqrt, bias=eps_ap[:, :]), sSE)
                wt("vector", sSE, s1_)
                v5 = em("vector", lambda v, h=h: v.reciprocal(rst_[h], rst_[h]), sVE)
                wt("vector", sVE, v5)
                em("vector", lambda v, h=h: v.tensor_mul(mr_[h], m_[h], rst_[h]), sVE)
            LN1_ROWS_VE = sVE.n
            wt("tensor", sVE, LN1_ROWS_VE)
            for h in range(2):
                em("tensor", lambda t, h=h: t.matmul(
                    pb[2 * h][:, :], ones_row[:, :], rst_[h], start=True, stop=True), None)
                em("tensor", lambda t, h=h: t.matmul(
                    pb[2 * h + 1][:, :], ones_row[:, :], mr_[h], start=True, stop=True),
                    sPE if h == 1 else None)
            LN1_BC_PE = sPE.n

            xf = [arena.view(o, [128, 1024], F32) for o in o_xf]
            xlo = [arena.view(o, [128, 512], BF16) for o in o_xlo]
            tmpv = arena.view(o_tmp, [128, 512], F32)
            wt("vector", sPE, LN1_BC_PE)
            last_xlo = [0, 0]
            last_v2 = 0
            for c in range(32):
                b = c % 2
                v = em("sync", lambda sy, c=c, b=b: sy.dma_start(
                    out=xf[b][:, :], in_=x_fm[c * 128:(c + 1) * 128, :]), sXF[b], 16)
                wt("vector", sXF[b], v)
                for h in range(2):
                    wt("vector", sVE, last_v2)
                    v1 = em("vector", lambda v_, b=b, h=h: v_.tensor_tensor(
                        tmpv[:, :], xf[b][:, h * 512:(h + 1) * 512],
                        pb[2 * h][:, :], ALU.mult), sVE)
                    wt("vector", sVE, v1)
                    wt("vector", sXLO[h], last_xlo[h])
                    v2 = em("vector", lambda v_, h=h: v_.tensor_tensor(
                        xlo[h][:, :], tmpv[:, :], pb[2 * h + 1][:, :], ALU.subtract), sVE)
                    last_v2 = v2
                    wt("sync", sVE, v2)
                    last_xlo[h] = em("sync", lambda sy, c=c, h=h: sy.dma_start(
                        out=xln_c[:, c * 1024 + h * 512: c * 1024 + (h + 1) * 512],
                        in_=xlo[h][:, :]), sXLO[h], 16)
            LN1_XLO = [sems["sXLO0"].n, sems["sXLO1"].n]
            LN1_VE = sVE.n

            pk1 = arena.view(o_pk, [128, 32, 64], I32)
            lsb1 = arena.view(o_lsb, [128, 2048], I32)
            scl1 = arena.view(o_scl, [128, 64], F32)
            dqf1 = arena.view(o_dqf, [128, 32, 2, 64], F32)
            dqb1 = arena.view(o_dqb, [128, 4096], BF16)
            stg1 = arena.view(o_stg, [128, 32, 128], BF16)
            pk2 = arena.view(o_pk, [128, 43, 64], I32)
            lsb2 = arena.view(o_lsb, [128, 2752], I32)
            scl2 = arena.view(o_scl, [128, 86], F32)
            dqf2 = arena.view(o_dqf, [128, 43, 2, 64], F32)
            dqb2 = arena.view(o_dqb, [128, 5504], BF16)
            stg2 = arena.view(o_stg, [128, 44, 128], BF16)

            prev = [0, 0, 0]

            def dequant_tile(kind, ti, wsrc, ssrc, cdst3):
                if kind in ("w1", "w3"):
                    ht = ti
                    hP = _hu(ht)
                    pk, lsbv, sclv, dqf, dqb, stg = pk1, lsb1, scl1, dqf1, dqb1, stg1
                    npk = 32 * hP
                    nfree = 2048
                    g_pk = lambda g: g.dma_start(
                        out=pk[0:hP, :, :],
                        in_=wsrc[4096 * ht: 4096 * ht + 32 * hP, :].rearrange(
                            "(p r) g -> p r g", r=32))
                    g_sc = lambda g: g.dma_start(
                        out=sclv[0:hP, :],
                        in_=ssrc[64 * 128 * ht: 64 * (128 * ht + hP)].rearrange(
                            "(p g) -> p g", g=64))
                    nP = hP
                    blocks = [(c * 128, 128, c) for c in range(32)]
                else:
                    dtile, half = ti
                    hP = 128
                    pk, lsbv, sclv, dqf, dqb, stg = pk2, lsb2, scl2, dqf2, dqb2, stg2
                    g_pk = lambda g: g.dma_start(
                        out=pk[:, :, :],
                        in_=wsrc[86 * 128 * dtile: 86 * 128 * (dtile + 1), :].rearrange(
                            "(p r) g -> p r g", r=86)[:, 43 * half: 43 * (half + 1), :])
                    g_sc = lambda g: g.dma_start(
                        out=sclv[:, :],
                        in_=s2_sl[172 * 128 * dtile: 172 * 128 * (dtile + 1)].rearrange(
                            "(p g) -> p g", g=172)[:, 86 * half: 86 * (half + 1)])
                    nP = 128
                    blocks = []
                    for u in range(44 * half, 44 * (half + 1)):
                        rtk, htk = u // NHT, u % NHT
                        start = HSL * rtk + 128 * htk - 5504 * half
                        blocks.append((start, _hu(htk), u - 44 * half))

                wt("scalar", sGE, CONST_GE)
                d1 = gdma(g_pk, extra_wait=(sVE, prev[0]) if prev[0] else None)
                d2 = gdma(g_sc)
                wt("vector", sGD, d2)
                wt("vector", sSE, prev[1])
                pkf = pk[0:nP, :, :].rearrange("p r g -> p (r g)")
                v1 = em("vector", lambda v: v.tensor_single_scalar(
                    lsbv[0:nP, :], pkf, 15, ALU.bitwise_and), sVE)
                wt("vector", sVE, v1)
                v2 = em("vector", lambda v: v.tensor_single_scalar(
                    lsbv[0:nP, :], lsbv[0:nP, :], 8, ALU.bitwise_xor), sVE)
                wt("vector", sVE, v2)
                v3 = em("vector", lambda v: v.tensor_single_scalar(
                    lsbv[0:nP, :], lsbv[0:nP, :], 8, ALU.subtract), sVE)
                wt("vector", sVE, v3)
                v4 = em("vector", lambda v: v.tensor_single_scalar(
                    pkf, pkf, 4, ALU.arith_shift_right), sVE)
                wt("vector", sVE, v4)
                v5 = em("vector", lambda v: v.tensor_copy(
                    dqf[0:nP, :, 0, :], pk[0:nP, :, :]), sVE)
                v6 = em("vector", lambda v: v.tensor_copy(
                    dqf[0:nP, :, 1, :],
                    lsbv[0:nP, :].rearrange("p (r g) -> p r g", g=64)), sVE)
                wt("vector", sVE, v6)
                nseg = dqf.shape[1]
                scb = sclv[0:nP, :].rearrange(
                    "p (r two) -> p r two", two=2).unsqueeze(3).to_broadcast(
                    [nP, nseg, 2, 64])
                v7 = em("vector", lambda v: v.tensor_tensor(
                    dqb[0:nP, :].rearrange("p (r two g) -> p r two g", two=2, g=64),
                    dqf[0:nP, :, :, :], scb, ALU.mult), sVE)
                prev[0] = v7
                wt("tensor", sVE, v7)
                wt("scalar", sDQ, prev[2])
                tr_marks = []
                for (cst, cl, cidx) in blocks:
                    bank = 6 + (cidx % 2)
                    if len(tr_marks) >= 2:
                        wt("tensor", sSE, tr_marks[-2])
                    em("tensor", lambda t, cst=cst, cl=cl, bank=bank, nP=nP: t.matmul(
                        pb[bank][0:cl, 0:nP], dqb[0:nP, cst:cst + cl],
                        ident[0:nP, 0:nP], start=True, stop=True), sPE)
                    pev = sPE.n
                    wt("scalar", sPE, pev)
                    se_ = em("scalar", lambda s, cidx=cidx, cl=cl, bank=bank, nP=nP, stg=stg: s.copy(
                        stg[0:cl, cidx, 0:nP], pb[bank][0:cl, 0:nP]), sSE)
                    tr_marks.append(se_)
                prev[1] = tr_marks[-1]
                wt("sync", sSE, prev[1])
                nst = stg.shape[1] if kind != "w2" else 44
                prev[2] = em("sync", lambda sy: sy.dma_start(
                    out=cdst3, in_=(stg1[:, :, :] if kind in ("w1", "w3")
                                    else stg2[:, 0:44, :])), sDQ, 16)

            w2t_c3 = w2t_c[:, :].rearrange("p (u d) -> p u d", d=W2D)
            for ht in range(NHT):
                dequant_tile("w1", ht, w1_sl, s1_sl,
                             w1t_c[:, ht * D:(ht + 1) * D].rearrange(
                                 "p (c j) -> p c j", j=128))
            for ht in range(NHT):
                dequant_tile("w3", ht, w3_sl, s3_sl,
                             w3t_c[:, ht * D:(ht + 1) * D].rearrange(
                                 "p (c j) -> p c j", j=128))
            for dtile in range(4):
                for half in range(2):
                    cdst3 = w2t_c3[:, 44 * half:44 * (half + 1),
                                   dtile * 128:(dtile + 1) * 128]
                    dequant_tile("w2", (dtile, half), w2_sl, s2_sl, cdst3)
            DEQ_DQ, DEQ_VE, DEQ_SE, DEQ_PE, DEQ_GD = sDQ.n, sVE.n, sSE.n, sPE.n, sGD.n

            wt("gpsimd", sXLO[0], LN1_XLO[0])
            wt("gpsimd", sXLO[1], LN1_XLO[1])
            em("gpsimd", lambda g: g.collective_compute(
                "AllGather", ALU.bypass, replica_groups=[list(range(NC))],
                ins=[xln_c[:, :]], outs=[xln_g[:, :]]), sCC)
            CC_XLN = sCC.n

            wt("gpsimd", sDQ, DEQ_DQ)
            em("gpsimd", lambda g: g.collective_compute(
                "AllGather", ALU.bypass, replica_groups=[list(range(NC))],
                ins=[w1t_c[:, :]], outs=[w1t_g[:, :]]), sCC)
            em("gpsimd", lambda g: g.collective_compute(
                "AllGather", ALU.bypass, replica_groups=[list(range(NC))],
                ins=[w3t_c[:, :]], outs=[w3t_g[:, :]]), sCC)
            em("gpsimd", lambda g: g.collective_compute(
                "AllGather", ALU.bypass, replica_groups=[list(range(NC))],
                ins=[w2t_c[:, :]], outs=[w2t_g[:, :]]), sCC)
            CC_W = sCC.n

            fence_all()
            if upto <= 1:
                dummy_out()
                return

            xl = [arena.view(o, [128, 32, 512], BF16) for o in o_xl]
            evv = arena.view(o_ev, [128, 4, 512], BF16)
            ev_free = [0] * 4
            qb_pe = [0] * 16
            xg4 = xln_g[:, :].rearrange("(r p) (c q) -> r p c q", p=128, q=T)
            for qb in range(16):
                b = qb % 2
                rsrc, csrc = qb // 2, (qb % 2) * 512
                if qb >= 2:
                    wt("sync", sPE, qb_pe[qb - 2])
                xld = em("sync", lambda sy, b=b, rsrc=rsrc, csrc=csrc: sy.dma_start(
                    out=xl[b][:, :, :],
                    in_=xg4[rsrc, :, :, csrc:csrc + 512]), sXL[b], 16)
                wt("tensor", sXL[b], xld)
                for wi, (w_v, tmaj) in enumerate(((wq_v, False), (wk_v, False),
                                                  (wv_v, True))):
                    for c in range(32):
                        for hj in range(4):
                            if not tmaj:
                                em("tensor", lambda t, w_v=w_v, c=c, hj=hj, b=b: t.matmul(
                                    pb[hj][:, :], w_v[:, c, hj * 128:(hj + 1) * 128],
                                    xl[b][:, c, :], start=(c == 0), stop=(c == 31)),
                                    sPE if (c == 31 and hj == 3) else None)
                            else:
                                em("tensor", lambda t, w_v=w_v, c=c, hj=hj, b=b: t.matmul(
                                    pb[hj][:, :], xl[b][:, c, hj * 128:(hj + 1) * 128],
                                    w_v[:, c, :], start=(c == 0), stop=(c == 31)),
                                    sPE if (c == 31 and hj == 3) else None)
                    pev = sPE.n
                    for hj in range(4):
                        wt("scalar", sPE, pev)
                        wt("scalar", sEV[hj], ev_free[hj])
                        sev = em("scalar", lambda s, hj=hj: s.copy(
                            evv[:, hj, :], pb[hj][:, :]), sSE)
                        wt("sync", sSE, sev)
                        if wi == 0:
                            ev_free[hj] = em("sync", lambda sy, hj=hj, qb=qb: sy.dma_start(
                                out=q_dram[hj * 128:(hj + 1) * 128, qb * 512:(qb + 1) * 512],
                                in_=evv[:, hj, :]), sEV[hj], 16)
                        elif wi == 1:
                            ev_free[hj] = em("sync", lambda sy, hj=hj, qb=qb: sy.dma_start(
                                out=k_dram[hj * 128:(hj + 1) * 128, qb * 512:(qb + 1) * 512],
                                in_=evv[:, hj, :]), sEV[hj], 16)
                        else:
                            ev_free[hj] = em("sync", lambda sy, hj=hj, qb=qb: sy.dma_start(
                                out=v_dram[qb * 512 + hj * 128: qb * 512 + (hj + 1) * 128, :],
                                in_=evv[:, hj, :]), sEV[hj], 16)
                    wt("tensor", sSE, sSE.n)
                qb_pe[qb] = sPE.n

            fence_all()
            if upto <= 2:
                dummy_out()
                return

            Kv = arena.view(o_K, [128, 4, 8192], BF16)
            Vv = arena.view(o_V, [128, 64, 512], BF16)
            Qv = [arena.view(o, [128, 512], BF16) for o in o_Q]
            Pv = [arena.view(o, [128, 512], BF16) for o in o_pr]
            Lb = arena.view(o_li, [128, 512], F32)
            Ao = [arena.view(o, [128, 512], BF16) for o in o_ao]
            lrow = rows[0:1, 4096:4608]
            kld = em("sync", lambda sy: sy.dma_start(
                out=Kv[:, :, :],
                in_=k_dram[:, :].rearrange("(hj p) k -> p hj k", p=128)), sKV, 16)
            vld = em("sync", lambda sy: sy.dma_start(
                out=Vv[:, :, :],
                in_=v_dram[:, :].rearrange("(tt p) e -> p tt e", p=128)), sKV, 16)
            sc_done = [0, 0]
            acc_done = [0, 0]
            ao_dma = [0, 0]
            q_dma = [0, 0]
            pe_hist = []
            it = 0
            for hj in range(4):
                for qb in range(16):
                    ab = it % 2
                    qbuf = it % 2
                    if it >= 2:
                        wt("sync", sPE, pe_hist[it - 2])
                    q_dma[qbuf] = em("sync", lambda sy, hj=hj, qb=qb, qbuf=qbuf: sy.dma_start(
                        out=Qv[qbuf][:, :],
                        in_=q_dram[hj * 128:(hj + 1) * 128, qb * 512:(qb + 1) * 512]),
                        sQ[qbuf], 16)
                    wt("tensor", sQ[qbuf], q_dma[qbuf])
                    if it < 2:
                        wt("tensor", sKV, vld)
                    wt("tensor", sVE, acc_done[ab])
                    nkt = 4 * qb + 4
                    for kt in range(nkt):
                        sb = kt % 2
                        wt("tensor", sSE, sc_done[sb])
                        pe1 = em("tensor", lambda t, hj=hj, kt=kt, sb=sb, qbuf=qbuf: t.matmul(
                            pb[sb][:, :], Kv[:, hj, kt * 128:(kt + 1) * 128],
                            Qv[qbuf][:, :], start=True, stop=True), sPE)
                        if kt >= 4 * qb:
                            mm = kt - 4 * qb
                            wt("vector", sPE, pe1)
                            mdone = em("vector", lambda v, sb=sb, mm=mm: v.tensor_tensor(
                                pb[sb][:, :], pb[sb][:, :],
                                mask_big[:, 384 - 128 * mm: 896 - 128 * mm], ALU.add), sVE)
                            wt("scalar", sVE, mdone)
                        else:
                            wt("scalar", sPE, pe1)
                        sc_done[sb] = em("scalar", lambda s, sb=sb: s.activation(
                            Pv[sb][:, :], pb[sb][:, :], AF.Exp,
                            bias=sb_b[:, :], scale=sc_c[:, :]), sSE)
                        wt("tensor", sSE, sc_done[sb])
                        em("tensor", lambda t, hj=hj, kt=kt, sb=sb, ab=ab, nkt=nkt: t.matmul(
                            pb[2 + ab][:, :], Vv[:, kt, hj * 128:(hj + 1) * 128],
                            Pv[sb][:, :], start=(kt == 0), stop=(kt == nkt - 1)), None)
                        em("tensor", lambda t, kt=kt, sb=sb, ab=ab, nkt=nkt: t.matmul(
                            pb[4 + ab][0:1, :], ones_col[:, :], Pv[sb][:, :],
                            start=(kt == 0), stop=(kt == nkt - 1)),
                            sPE if kt == nkt - 1 else None)
                    pe_end = sPE.n
                    wt("vector", sPE, pe_end)
                    v1 = em("vector", lambda v, ab=ab: v.reciprocal(
                        lrow, pb[4 + ab][0:1, :]), sVE)
                    wt("tensor", sVE, v1)
                    p_bc = em("tensor", lambda t: t.matmul(
                        pb[6][:, :], ones_row[:, :], lrow, start=True, stop=True), sPE)
                    pe_hist.append(sPE.n)
                    wt("vector", sPE, p_bc)
                    wt("vector", sVE, acc_done[(it - 1) % 2] if it >= 1 else 0)
                    v2 = em("vector", lambda v: v.tensor_copy(Lb[:, :], pb[6][:, :]), sVE)
                    wt("vector", sVE, v2)
                    wt("vector", sATs[qbuf], ao_dma[qbuf])
                    acc_done[ab] = em("vector", lambda v, ab=ab, qbuf=qbuf: v.tensor_tensor(
                        Ao[qbuf][:, :], pb[2 + ab][:, :], Lb[:, :], ALU.mult), sVE)
                    wt("sync", sVE, acc_done[ab])
                    ao_dma[qbuf] = em("sync", lambda sy, hj=hj, qb=qb, qbuf=qbuf: sy.dma_start(
                        out=attn_c[hj * 128:(hj + 1) * 128, qb * 512:(qb + 1) * 512],
                        in_=Ao[qbuf][:, :]), sATs[qbuf], 16)
                    it += 1

            wt("gpsimd", sATs[0], sems["sAT0"].n)
            wt("gpsimd", sATs[1], sems["sAT1"].n)
            em("gpsimd", lambda g: g.collective_compute(
                "AllGather", ALU.bypass, replica_groups=[list(range(NC))],
                ins=[attn_c[:, :]], outs=[attn_g[:, :]]), sCC)
            CC_AT = sCC.n

            fence_all()
            if upto <= 3:
                dummy_out()
                return

            arv = arena.view(o_ar, [128, 32, 512], BF16)
            wov = [arena.view(o, [128, 32, 256], BF16) for o in o_wo]
            x1s = arena.view(o_x1s, [128, 32, 512], F32)
            xtv = [arena.view(o, [128, 512], F32) for o in o_xt]
            x1b = arena.view(o_b1, [128, 512], BF16)
            xqb = arena.view(o_b2, [128, 512], BF16)
            t2v = arena.view(o_t2, [128, 512], F32)
            xl2o = [arena.view(o, [128, 512], BF16) for o in o_xl2o]
            ag3 = attn_g[:, :].rearrange("(ec p) q -> p ec q", p=128)
            m2 = rows[0:1, 0:512]
            e2 = rows[0:1, 512:1024]
            va2 = rows[0:1, 1024:1536]
            rs2 = rows[0:1, 1536:2048]
            mr2 = rows[0:1, 2048:2560]
            wo_pe_h = [0, 0]
            last_wo_pe = 0
            xt_ve = [0, 0]
            x1s_mark = 0
            xl2_st = [0, 0]
            for qt in range(2):
                wt("sync", sPE, last_wo_pe)
                wt("vector", sX1S, x1s_mark)
                ar_d = em("sync", lambda sy, qt=qt: sy.dma_start(
                    out=arv[:, :, :],
                    in_=ag3[:, :, bass.ds((E_.rank * T + qt * 512) if E_.name == "sync"
                                          else qt * 512, 512)]), sAR, 16)
                st_pe = []
                for dtp in range(16):
                    wb = dtp % 2
                    wt("scalar", sPE, wo_pe_h[wb])
                    wo_d = em("scalar", lambda s, dtp=dtp, wb=wb: s.dma_start(
                        out=wov[wb][:, :, :],
                        in_=wo_bf[:, dtp * 256:(dtp + 1) * 256].rearrange(
                            "(ec p) d -> p ec d", p=128)), sWO[wb], 16)
                    wt("tensor", sWO[wb], wo_d)
                    wt("tensor", sAR, ar_d)
                    for dh in range(2):
                        dt = dtp * 2 + dh
                        bank = dt % 2
                        wt("tensor", sVE, xt_ve[bank])
                        for ec in range(32):
                            em("tensor", lambda t, wb=wb, dh=dh, ec=ec, bank=bank: t.matmul(
                                pb[bank][:, :], wov[wb][:, ec, dh * 128:(dh + 1) * 128],
                                arv[:, ec, :], start=(ec == 0), stop=(ec == 31)),
                                sPE if ec == 31 else None)
                        pev = sPE.n
                        xb_ = dt % 2
                        wt("sync", sVE, xt_ve[xb_])
                        xd = em("sync", lambda sy, dt=dt, qt=qt, xb_=xb_: sy.dma_start(
                            out=xtv[xb_][:, :],
                            in_=x_fm[dt * 128:(dt + 1) * 128, qt * 512:(qt + 1) * 512]),
                            sXT[xb_], 16)
                        wt("vector", sPE, pev)
                        wt("vector", sXT[xb_], xd)
                        xt_ve[xb_] = em("vector", lambda v, dt=dt, bank=bank, xb_=xb_: v.tensor_tensor(
                            x1s[:, dt, :], pb[bank][:, :], xtv[xb_][:, :], ALU.add), sVE)
                        v1 = xt_ve[xb_]
                        wt("scalar", sVE, v1)
                        if dt >= 1:
                            wt("scalar", sPE, st_pe[dt - 1])
                        s1_ = em("scalar", lambda s, dt=dt: s.copy(
                            x1b[:, :], x1s[:, dt, :]), sSE)
                        wt("scalar", sSE, s1_)
                        s2_ = em("scalar", lambda s, dt=dt: s.activation(
                            xqb[:, :], x1s[:, dt, :], AF.Square), sSE)
                        wt("tensor", sSE, s2_)
                        em("tensor", lambda t, dt=dt: t.matmul(
                            pb[2][0:1, :], ones_col[:, :], x1b[:, :],
                            start=(dt == 0), stop=(dt == 31)), None)
                        pe2 = em("tensor", lambda t, dt=dt: t.matmul(
                            pb[3][0:1, :], ones_col[:, :], xqb[:, :],
                            start=(dt == 0), stop=(dt == 31)), sPE)
                        st_pe.append(pe2)
                        wt("sync", sVE, v1)
                        em("sync", lambda sy, dt=dt, qt=qt: sy.dma_start(
                            out=x1_dram[dt * 128:(dt + 1) * 128, qt * 512:(qt + 1) * 512],
                            in_=x1s[:, dt, :]), sX1S, 16)
                    wo_pe_h[wb] = sPE.n
                last_wo_pe = sPE.n
                x1s_mark = sX1S.n
                wt("vector", sPE, st_pe[-1])
                em("vector", lambda v: v.tensor_scalar_mul(m2, pb[2][0:1, :], INV_D), sVE)
                v2_ = em("vector", lambda v: v.tensor_scalar_mul(e2, pb[3][0:1, :], INV_D), sVE)
                wt("vector", sVE, v2_)
                v3_ = em("vector", lambda v: v.tensor_mul(va2, m2, m2), sVE)
                wt("vector", sVE, v3_)
                v4_ = em("vector", lambda v: v.tensor_sub(va2, e2, va2), sVE)
                wt("scalar", sVE, v4_)
                s3_ = em("scalar", lambda s: s.activation(
                    rs2, va2, AF.Sqrt, bias=eps_ap[:, :]), sSE)
                wt("vector", sSE, s3_)
                v5_ = em("vector", lambda v: v.reciprocal(rs2, rs2), sVE)
                wt("vector", sVE, v5_)
                v6_ = em("vector", lambda v: v.tensor_mul(mr2, m2, rs2), sVE)
                wt("tensor", sVE, v6_)
                em("tensor", lambda t: t.matmul(
                    pb[4][:, :], ones_row[:, :], rs2, start=True, stop=True), None)
                bc2 = em("tensor", lambda t: t.matmul(
                    pb[5][:, :], ones_row[:, :], mr2, start=True, stop=True), sPE)
                wt("vector", sPE, bc2)
                last_v8 = 0
                for dt in range(32):
                    ob = dt % 2
                    wt("vector", sVE, last_v8)
                    v7 = em("vector", lambda v, dt=dt: v.tensor_tensor(
                        t2v[:, :], x1s[:, dt, :], pb[4][:, :], ALU.mult), sVE)
                    wt("vector", sVE, v7)
                    wt("vector", sXO[ob], xl2_st[ob])
                    v8 = em("vector", lambda v, ob=ob: v.tensor_tensor(
                        xl2o[ob][:, :], t2v[:, :], pb[5][:, :], ALU.subtract), sVE)
                    last_v8 = v8
                    wt("sync", sVE, v8)
                    xl2_st[ob] = em("sync", lambda sy, dt=dt, qt=qt, ob=ob: sy.dma_start(
                        out=xl2_dram[dt * 128:(dt + 1) * 128, qt * 512:(qt + 1) * 512],
                        in_=xl2o[ob][:, :]), sXO[ob], 16)

            fence_all()
            if upto <= 4:
                dummy_out()
                return

            hact = arena.view(o_hact, [128, NSLOT, 512], BF16)
            xl2v = arena.view(o_xl2, [128, 32, 512], BF16)
            w13 = [arena.view(o, [128, 32, 128], BF16) for o in o_w13]
            htmp = arena.view(o_ht, [128, 512], F32)
            w2b = [arena.view(o, [128, 22, 512], BF16) for o in o_w2b]
            x1t = [arena.view(o, [128, 512], F32) for o in o_x1t]
            osb = [arena.view(o, [128, 512], F32) for o in o_os]
            w2g3 = [w2t_g[128 * rd:128 * (rd + 1), :].rearrange(
                "p (u d) -> p u d", d=512) for rd in range(NC)]
            out_st = [0, 0]
            ab_pe_h = [0, 0]
            hact_ve_h = [0, 0]
            w2_pe_h = [0, 0]
            prev_qt = {"pe": 0, "ve": 0, "st0": 0, "st1": 0}
            for qt in range(2):
                for eng in ("sync", "scalar"):
                    wt(eng, sPE, prev_qt["pe"])
                    wt(eng, sVE, prev_qt["ve"])
                    wt(eng, sSTs[0], prev_qt["st0"])
                    wt(eng, sSTs[1], prev_qt["st1"])
                x2d = em("sync", lambda sy, qt=qt: sy.dma_start(
                    out=xl2v[:, :, :],
                    in_=xl2_dram[:, :].rearrange("(c p) q -> p c q", p=128)[
                        :, :, qt * 512:(qt + 1) * 512]), sX2, 16)
                for u in range(NSLOT):
                    ru, htu = u // NHT, u % NHT
                    hu = _hu(htu)
                    b1_, b3_ = (u % 2) * 2, (u % 2) * 2 + 1
                    wbuf1, wbuf3 = (u % 2), 2 + (u % 2)
                    wt("scalar", sPE, ab_pe_h[u % 2])
                    em("scalar", lambda s, ru=ru, htu=htu, wbuf1=wbuf1: s.dma_start(
                        out=w13[wbuf1][:, :, :],
                        in_=w1t_g[128 * ru:128 * (ru + 1),
                                  htu * D:(htu + 1) * D].rearrange(
                            "p (c j) -> p c j", j=128)), sW13[wbuf1], 16)
                    wd = em("scalar", lambda s, ru=ru, htu=htu, wbuf3=wbuf3: s.dma_start(
                        out=w13[wbuf3][:, :, :],
                        in_=w3t_g[128 * ru:128 * (ru + 1),
                                  htu * D:(htu + 1) * D].rearrange(
                            "p (c j) -> p c j", j=128)), sW13[wbuf3], 16)
                    wt("tensor", sW13[wbuf1], sems[f"sW13_{wbuf1}"].n)
                    wt("tensor", sW13[wbuf3], wd)
                    wt("tensor", sX2, x2d)
                    wt("tensor", sVE, hact_ve_h[u % 2])
                    for c in range(32):
                        em("tensor", lambda t, c=c, hu=hu, b1_=b1_, wbuf1=wbuf1: t.matmul(
                            pb[b1_][0:hu, :], w13[wbuf1][:, c, 0:hu], xl2v[:, c, :],
                            start=(c == 0), stop=(c == 31)), None)
                        em("tensor", lambda t, c=c, hu=hu, b3_=b3_, wbuf3=wbuf3: t.matmul(
                            pb[b3_][0:hu, :], w13[wbuf3][:, c, 0:hu], xl2v[:, c, :],
                            start=(c == 0), stop=(c == 31)),
                            sPE if c == 31 else None)
                    ab_pe_h[u % 2] = sPE.n
                    wt("scalar", sPE, ab_pe_h[u % 2])
                    if u >= 1:
                        wt("scalar", sVE, hact_ve_h[(u - 1) % 2])
                    sl = em("scalar", lambda s, hu=hu, b1_=b1_: s.activation(
                        htmp[0:hu, :], pb[b1_][0:hu, :], AF.Silu), sSE)
                    wt("vector", sSE, sl)
                    hact_ve_h[u % 2] = em("vector", lambda v, u=u, hu=hu, b3_=b3_: v.tensor_tensor(
                        hact[0:hu, u, :], pb[b3_][0:hu, :], htmp[0:hu, :], ALU.mult), sVE)
                AB_VE = sVE.n
                AB_PE = sPE.n
                wt("tensor", sVE, AB_VE)
                wt("scalar", sPE, AB_PE)
                wt("scalar", sVE, AB_VE)
                wt("sync", sPE, AB_PE)
                wt("sync", sVE, AB_VE)
                w2_evac_ve = 0
                for rd in range(NC):
                    wt("tensor", sVE, w2_evac_ve)
                    for usub in range(4):
                        wb = usub % 2
                        wt("scalar", sPE, w2_pe_h[wb])
                        w2_d = em("scalar", lambda s, rd=rd, usub=usub, wb=wb: s.dma_start(
                            out=w2b[wb][:, :, :],
                            in_=w2g3[rd][:, 22 * usub:22 * (usub + 1), :]), sW2[wb], 16)
                        wt("tensor", sW2[wb], w2_d)
                        for ui in range(22):
                            u = usub * 22 + ui
                            hu = _hu(u % NHT)
                            for dj in range(4):
                                em("tensor", lambda t, wb=wb, ui=ui, dj=dj, u=u, hu=hu: t.matmul(
                                    pb[4 + dj][:, :],
                                    w2b[wb][0:hu, ui, dj * 128:(dj + 1) * 128],
                                    hact[0:hu, u, :],
                                    start=(u == 0), stop=(u == NSLOT - 1)),
                                    sPE if (ui == 21 and dj == 3) else None)
                        w2_pe_h[wb] = sPE.n
                    pev = sPE.n
                    for dj in range(4):
                        dt = rd * 4 + dj
                        xb_ = dj % 2
                        xd = em("sync", lambda sy, dt=dt, qt=qt, xb_=xb_: sy.dma_start(
                            out=x1t[xb_][:, :],
                            in_=x1_dram[dt * 128:(dt + 1) * 128,
                                        qt * 512:(qt + 1) * 512]), sX1T[xb_], 16)
                        ob = dj % 2
                        wt("vector", sPE, pev)
                        wt("vector", sX1T[xb_], xd)
                        wt("vector", sSTs[ob], out_st[ob])
                        vv = em("vector", lambda v, dj=dj, xb_=xb_, ob=ob: v.tensor_tensor(
                            osb[ob][:, :], pb[4 + dj][:, :], x1t[xb_][:, :], ALU.add), sVE)
                        wt("sync", sVE, vv)
                        out_st[ob] = em("sync", lambda sy, dt=dt, qt=qt, ob=ob: sy.dma_start(
                            out=out_fm[dt * 128:(dt + 1) * 128, qt * 512:(qt + 1) * 512],
                            in_=osb[ob][:, :]), sSTs[ob], 16)
                    w2_evac_ve = sVE.n
                prev_qt = {"pe": sPE.n, "ve": sVE.n,
                           "st0": sSTs[0].n, "st1": sSTs[1].n}

            wt("sync", sSTs[0], sems["sST0"].n)
            wt("sync", sSTs[1], sems["sST1"].n)
            wt("gpsimd", sSTs[0], sems["sST0"].n)
            wt("gpsimd", sSTs[1], sems["sST1"].n)

        for name, setter in (("gpsimd", blk.gpsimd), ("scalar", blk.scalar),
                             ("tensor", blk.tensor), ("vector", blk.vector),
                             ("sync", blk.sync)):
            def body(obj, name=name):
                program(E(name, obj))
            setter(body)

    return nc


_NC_CACHE = None


def _get_nc():
    global _NC_CACHE
    if _NC_CACHE is None:
        _NC_CACHE = build_nc()
    return _NC_CACHE


def make_in_maps(inputs):
    x = np.asarray(inputs["L0_attn_hidden_states"])
    wq = np.asarray(inputs["L0_attn_wq"])
    wkv = np.asarray(inputs["L0_attn_wkv"])
    wo = np.asarray(inputs["L0_attn_wo"])
    w1 = np.asarray(inputs["L0_ffn_w1"])
    s1 = np.asarray(inputs["L0_ffn_s1"])
    w2 = np.asarray(inputs["L0_ffn_w2"])
    s2 = np.asarray(inputs["L0_ffn_s2"])
    w3 = np.asarray(inputs["L0_ffn_w3"])
    s3 = np.asarray(inputs["L0_ffn_s3"])

    perm = [r + 8 * j for r in range(8) for j in range(4)]
    wo_p = np.ascontiguousarray(wo.reshape(32, 128, D)[perm].reshape(D, D))

    in_maps = []
    for r in range(NC):
        heads = [r + 8 * j for j in range(HJ)]
        wq_sl = np.ascontiguousarray(
            np.concatenate([wq[:, h * 192:h * 192 + 128] for h in heads], axis=1))
        wk_sl = np.ascontiguousarray(
            np.concatenate([wkv[:, h * 128:h * 128 + 128] for h in heads], axis=1))
        wv_sl = np.ascontiguousarray(
            np.concatenate([wkv[:, D + h * 128:D + h * 128 + 128] for h in heads],
                           axis=1))
        x_fm = np.ascontiguousarray(x[0, r * T:(r + 1) * T, :].T)
        in_maps.append({
            "x_fm": x_fm, "wq_sl": wq_sl, "wk_sl": wk_sl, "wv_sl": wv_sl,
            "wo_p": wo_p,
            "w1_sl": np.ascontiguousarray(w1[32 * HSL * r: 32 * HSL * (r + 1)]),
            "s1_sl": np.ascontiguousarray(s1[64 * HSL * r: 64 * HSL * (r + 1)]),
            "w3_sl": np.ascontiguousarray(w3[32 * HSL * r: 32 * HSL * (r + 1)]),
            "s3_sl": np.ascontiguousarray(s3[64 * HSL * r: 64 * HSL * (r + 1)]),
            "w2_sl": np.ascontiguousarray(w2[86 * W2D * r: 86 * W2D * (r + 1)]),
            "s2_sl": np.ascontiguousarray(s2[172 * W2D * r: 172 * W2D * (r + 1)]),
        })
    return in_maps


def kernel(**inputs):
    in_maps = make_in_maps(inputs)
    nc = _get_nc()
    res = run_bass_kernel_spmd(nc, in_maps, core_ids=list(range(NC)))
    out = np.empty((S, D), np.float32)
    for r in range(NC):
        out[r * T:(r + 1) * T] = res.results[r]["out_fm"].T
    return out.reshape(1, S, D)


# version 05
# speedup vs baseline: 1.3630x; 1.3630x over previous
import numpy as np
from contextlib import ExitStack

import concourse.bass as bass
import concourse.mybir as mybir
from concourse.bass_utils import run_bass_kernel_spmd

F32 = mybir.dt.float32
BF16 = mybir.dt.bfloat16
I32 = mybir.dt.int32
U8 = mybir.dt.uint8
AF = mybir.ActivationFunctionType
ALU = mybir.AluOpType

S = 8192
D = 4096
NC = 8
T = S // NC
HJ = 4
HID = 11008
HSL = HID // NC
NHT = 11
NSLOT = NC * NHT
W2D = D // NC
CSCALE = float(192.0 ** -0.5)
EXP_BIAS = -16.0
EPS = 1e-5
INV_D = 1.0 / D

ENGINES = ("sync", "gpsimd", "vector", "scalar", "tensor")


def _hu(ht):
    return 96 if ht == NHT - 1 else 128


class Sem:
    def __init__(self, ctx, nc, name):
        self.h = ctx.enter_context(nc.semaphore(name))
        self.n = 0


class Arena:
    def __init__(self, tensor, nbytes):
        self.t = tensor
        self.nbytes = nbytes

    def view(self, off, shape, dtype):
        esz = {F32: 4, BF16: 2, I32: 4}[dtype]
        n = 1
        for s in shape[1:]:
            n *= s
        assert off % 32 == 0 and off + n * esz <= self.nbytes, (off, n * esz)
        ap = self.t[0:shape[0], off:off + n * esz].bitcast(dtype)
        if len(shape) == 2:
            return ap
        pat_in = "p (" + " ".join(f"a{i}" for i in range(1, len(shape))) + ")"
        pat_out = "p " + " ".join(f"a{i}" for i in range(1, len(shape)))
        kw = {f"a{i}": shape[i] for i in range(1, len(shape))}
        return ap.rearrange(f"{pat_in} -> {pat_out}", **kw)


class E:
    def __init__(self, name, obj):
        self.name = name
        self.obj = obj
        self.rank = None

    def emit(self, eng, fn, sem=None, inc=1):
        if eng == self.name:
            inst = fn(self.obj)
            if sem is not None:
                inst.then_inc(sem.h, inc)
        if sem is not None:
            sem.n += inc
            return sem.n
        return None

    def wait(self, eng, sem, val):
        if val > 0 and eng == self.name:
            self.obj.wait_ge(sem.h, val)


def build_nc(upto=9):
    nc = bass.Bass("TRN2", target_bir_lowering=False)

    x_fm = nc.dram_tensor("x_fm", [D, T], F32, kind="ExternalInput")
    wq_sl = nc.dram_tensor("wq_sl", [D, 512], F32, kind="ExternalInput")
    wk_sl = nc.dram_tensor("wk_sl", [D, 512], F32, kind="ExternalInput")
    wv_sl = nc.dram_tensor("wv_sl", [D, 512], F32, kind="ExternalInput")
    wo_p = nc.dram_tensor("wo_p", [D, D], F32, kind="ExternalInput")
    w1_sl = nc.dram_tensor("w1_sl", [32 * HSL, 64], I32, kind="ExternalInput")
    s1_sl = nc.dram_tensor("s1_sl", [64 * HSL], F32, kind="ExternalInput")
    w3_sl = nc.dram_tensor("w3_sl", [32 * HSL, 64], I32, kind="ExternalInput")
    s3_sl = nc.dram_tensor("s3_sl", [64 * HSL], F32, kind="ExternalInput")
    w2_sl = nc.dram_tensor("w2_sl", [86 * W2D, 64], I32, kind="ExternalInput")
    s2_sl = nc.dram_tensor("s2_sl", [172 * W2D], F32, kind="ExternalInput")
    out_fm = nc.dram_tensor("out_fm", [D, T], F32, kind="ExternalOutput")

    xln_c = nc.dram_tensor("xln_c", [128, 32 * T], BF16)
    xln_g = nc.dram_tensor("xln_g", [128 * NC, 32 * T], BF16)
    attn_c = nc.dram_tensor("attn_c", [512, S], BF16)
    attn_g = nc.dram_tensor("attn_g", [512 * NC, S], BF16)
    w1t_c = nc.dram_tensor("w1t_c", [128, NHT * D], BF16)
    w1t_g = nc.dram_tensor("w1t_g", [128 * NC, NHT * D], BF16)
    w3t_c = nc.dram_tensor("w3t_c", [128, NHT * D], BF16)
    w3t_g = nc.dram_tensor("w3t_g", [128 * NC, NHT * D], BF16)
    w2t_c = nc.dram_tensor("w2t_c", [128, NSLOT * W2D], BF16)
    w2t_g = nc.dram_tensor("w2t_g", [128 * NC, NSLOT * W2D], BF16)
    q_dram = nc.dram_tensor("q_dram", [512, S], BF16)
    k_dram = nc.dram_tensor("k_dram", [512, S], BF16)
    v_dram = nc.dram_tensor("v_dram", [S, 512], BF16)
    wo_bf = nc.dram_tensor("wo_bf", [D, D], BF16)
    x1_dram = nc.dram_tensor("x1_dram", [D, T], F32)
    xl2_dram = nc.dram_tensor("xl2_dram", [128, 2 * 32 * 512], BF16)

    ctx = ExitStack()
    with ctx:
        mask_big = ctx.enter_context(nc.sbuf_tensor("mask_big", [128, 896], F32))
        ident = ctx.enter_context(nc.sbuf_tensor("ident", [128, 128], BF16))
        ones_col = ctx.enter_context(nc.sbuf_tensor("ones_col", [128, 1], BF16))
        ones_row = ctx.enter_context(nc.sbuf_tensor("ones_row", [1, 128], F32))
        sc_c = ctx.enter_context(nc.sbuf_tensor("sc_c", [128, 1], F32))
        sb_b = ctx.enter_context(nc.sbuf_tensor("sb_b", [128, 1], F32))
        eps_ap = ctx.enter_context(nc.sbuf_tensor("eps_ap", [1, 1], F32))
        rows = ctx.enter_context(nc.sbuf_tensor("rows", [1, 9 * 512], F32))

        NB = (nc.sbuf_bytes_remaining - 64) // 32 * 32
        arena = Arena(ctx.enter_context(nc.sbuf_tensor("arena", [128, NB], U8)), NB)

        pb = [ctx.enter_context(nc.psum_tensor(f"pb{i}", [128, 512], F32))
              for i in range(8)]

        sems = {}
        semnames = ["sLD", "sGD", "sPE", "sVE", "sSE", "sGE", "sCC",
                    "sXF0", "sXF1", "sXLO0", "sXLO1", "sDQ",
                    "sXL0", "sXL1", "sEV0", "sEV1", "sEV2", "sEV3",
                    "sKV", "sQ0", "sQ1", "sAT0", "sAT1",
                    "sAR", "sWO0", "sWO1", "sWO2", "sXT0", "sXT1", "sXO0", "sXO1",
                    "sX2", "sW13_0", "sW13_1", "sW13_2", "sW13_3", "sW13_4", "sW13_5",
                    "sW2_0", "sW2_1", "sW2_2", "sX1T0", "sX1T1", "sST0", "sST1", "sX1S"]
        for name in semnames:
            sems[name] = Sem(ctx, nc, name)

        blk = ctx.enter_context(nc.Block())

        off = [0]

        def al(nbytes, align=32):
            o = (off[0] + align - 1) // align * align
            off[0] = o + nbytes
            assert off[0] <= NB, (off[0], NB)
            return o

        o_xf = [al(4096), al(4096)]
        o_xb = [al(2048), al(2048)]
        o_xsq = al(2048)
        o_xlo = [al(1024), al(1024)]
        o_tmp = al(2048)
        o_pk = al(11008)
        o_lsb = al(11008)
        o_scl = al(704)
        o_dqf = al(22016)
        o_dqb = al(11008)
        o_stg = al(11264)
        o_wq = al(32768)
        o_wk = al(32768)
        o_wv = al(32768)

        off[0] = 0
        o_xl = [al(32768), al(32768)]
        o_ev = al(4 * 1024)
        assert off[0] <= o_wq

        off[0] = 0
        o_K = al(65536)
        o_V = al(65536)
        o_Q = [al(1024), al(1024)]
        o_pr = [al(1024), al(1024), al(1024)]
        o_li = al(2048)
        o_ao = [al(1024), al(1024)]

        off[0] = 0
        o_ar = al(32768)
        o_wo = [al(16384), al(16384), al(16384)]
        o_x1s = al(65536)
        o_xt = [al(2048), al(2048)]
        o_b1 = al(1024)
        o_b2 = al(1024)
        o_t2 = al(2048)
        o_xl2o = [al(1024), al(1024)]

        off[0] = 0
        o_hact = al(90112)
        o_xl2 = al(32768)
        o_w13 = [al(8192) for _ in range(6)]
        o_ht = al(2048)
        ffn2_base = o_xl2
        off[0] = ffn2_base
        o_w2b = [al(22528), al(22528), al(22528)]
        o_x1t = [al(2048), al(2048)]
        o_os = [al(2048), al(2048)]

        def program(E_):
            _dummy_done = [False]
            for s_ in sems.values():
                s_.n = 0
            em, wt = E_.emit, E_.wait
            sLD, sGD, sPE, sVE, sSE, sGE, sCC = (
                sems["sLD"], sems["sGD"], sems["sPE"], sems["sVE"], sems["sSE"],
                sems["sGE"], sems["sCC"])
            sXF = [sems["sXF0"], sems["sXF1"]]
            sXLO = [sems["sXLO0"], sems["sXLO1"]]
            sDQ = sems["sDQ"]
            sXL = [sems["sXL0"], sems["sXL1"]]
            sEV = [sems[f"sEV{i}"] for i in range(4)]
            sKV = sems["sKV"]
            sQ = [sems["sQ0"], sems["sQ1"]]
            sATs = [sems["sAT0"], sems["sAT1"]]
            sAR = sems["sAR"]
            sWO = [sems["sWO0"], sems["sWO1"], sems["sWO2"]]
            sXT = [sems["sXT0"], sems["sXT1"]]
            sXO = [sems["sXO0"], sems["sXO1"]]
            sX2 = sems["sX2"]
            sW13 = [sems[f"sW13_{i}"] for i in range(6)]
            sW2 = [sems["sW2_0"], sems["sW2_1"], sems["sW2_2"]]
            sX1T = [sems["sX1T0"], sems["sX1T1"]]
            sSTs = [sems["sST0"], sems["sST1"]]
            sX1S = sems["sX1S"]

            def gdma(fn, extra_wait=None):
                wt("gpsimd", sGD, sGD.n)
                if extra_wait is not None:
                    wt("gpsimd", extra_wait[0], extra_wait[1])
                return em("gpsimd", fn, sGD, 16)

            if E_.name == "sync":
                E_.rank = nc.partition_id()

            def fence_all():
                snap = [(s_, s_.n) for s_ in sems.values()]
                for eng in ENGINES:
                    for sem, val in snap:
                        wt(eng, sem, val)

            def dummy_out():
                _dummy_done[0] = True
                sSTs = [sems["sST0"], sems["sST1"]]
                for i in range(64):
                    dt, qh = i // 2, i % 2
                    em("sync", lambda sy, dt=dt, qh=qh: sy.dma_start(
                        out=out_fm[dt * 128:(dt + 1) * 128, qh * 512:(qh + 1) * 512],
                        in_=mask_big[:, 0:512]), sSTs[i % 2], 16)
                wt("sync", sSTs[0], sSTs[0].n)
                wt("sync", sSTs[1], sSTs[1].n)
                wt("gpsimd", sSTs[0], sSTs[0].n)
                wt("gpsimd", sSTs[1], sSTs[1].n)

            em("gpsimd", lambda g: g.memset(mask_big[:, :], 0.0), sGE)
            wt("gpsimd", sGE, sGE.n)
            em("gpsimd", lambda g: g.affine_select(
                mask_big[:, :], mask_big[:, :], [[1, 896]], ALU.is_ge, -1.0e30,
                base=-384, channel_multiplier=-1), sGE)
            em("gpsimd", lambda g: g.memset(ident[:, :], 1.0), sGE)
            wt("gpsimd", sGE, sGE.n)
            em("gpsimd", lambda g: g.affine_select(
                ident[:, :], ident[:, :], [[1, 128]], ALU.is_equal, 0.0,
                base=0, channel_multiplier=-1), sGE)
            em("gpsimd", lambda g: g.memset(ones_col[:, :], 1.0), sGE)
            em("gpsimd", lambda g: g.memset(ones_row[:, :], 1.0), sGE)
            em("gpsimd", lambda g: g.memset(sc_c[:, :], CSCALE), sGE)
            em("gpsimd", lambda g: g.memset(sb_b[:, :], EXP_BIAS), sGE)
            em("gpsimd", lambda g: g.memset(eps_ap[:, :], EPS), sGE)
            stg_all = arena.view(o_stg, [128, 5632], BF16)
            em("gpsimd", lambda g: g.memset(stg_all[:, :], 0.0), sGE)
            CONST_GE = sGE.n

            wq_v = arena.view(o_wq, [128, 32, 512], BF16)
            wk_v = arena.view(o_wk, [128, 32, 512], BF16)
            wv_v = arena.view(o_wv, [128, 32, 512], BF16)
            gdma(lambda g: g.dma_start(
                out=wq_v[:, :, :],
                in_=wq_sl[:, :].rearrange("(c p) e -> p c e", p=128)))
            gdma(lambda g: g.dma_start(
                out=wk_v[:, :, :],
                in_=wk_sl[:, :].rearrange("(c p) e -> p c e", p=128)))
            gdma(lambda g: g.dma_start(
                out=wv_v[:, :, :],
                in_=wv_sl[:, :].rearrange("(c p) e -> p c e", p=128)))
            W_CAST = sGD.n
            gdma(lambda g: g.dma_start(out=wo_bf[:, :], in_=wo_p[:, :]))
            WO_CAST = sGD.n

            xb = [arena.view(o, [128, 1024], BF16) for o in o_xb]
            xsq = arena.view(o_xsq, [128, 1024], BF16)
            c_pe = [0] * 32
            for c in range(32):
                b = c % 2
                v = gdma((lambda g, c=c, b=b: g.dma_start(
                    out=xb[b][:, :], in_=x_fm[c * 128:(c + 1) * 128, :])),
                    extra_wait=(sPE, c_pe[c - 2]) if c >= 2 else None)
                wt("scalar", sGD, v)
                if c >= 1:
                    wt("scalar", sPE, c_pe[c - 1])
                sq = em("scalar", lambda s, b=b: s.activation(
                    xsq[:, :], xb[b][:, :], AF.Square), sSE)
                wt("tensor", sSE, sq)
                if c == 0:
                    wt("tensor", sGE, CONST_GE)
                for h in range(2):
                    em("tensor", lambda t, b=b, h=h, c=c: t.matmul(
                        pb[2 * h][0:1, :], ones_col[:, :],
                        xb[b][:, h * 512:(h + 1) * 512],
                        start=(c == 0), stop=(c == 31)), None)
                    last = (h == 1)
                    em("tensor", lambda t, h=h, c=c: t.matmul(
                        pb[2 * h + 1][0:1, :], ones_col[:, :],
                        xsq[:, h * 512:(h + 1) * 512],
                        start=(c == 0), stop=(c == 31)), sPE if last else None)
                c_pe[c] = sPE.n
            LN1_STATS_PE = sPE.n

            m_ = [rows[0:1, 0:512], rows[0:1, 512:1024]]
            ex2_ = [rows[0:1, 1024:1536], rows[0:1, 1536:2048]]
            var_ = [rows[0:1, 2048:2560], rows[0:1, 2560:3072]]
            rst_ = [rows[0:1, 3072:3584], rows[0:1, 3584:4096]]
            mr_ = ex2_
            wt("vector", sPE, LN1_STATS_PE)
            for h in range(2):
                em("vector", lambda v, h=h: v.tensor_scalar_mul(
                    m_[h], pb[2 * h][0:1, :], INV_D), sVE)
                v2 = em("vector", lambda v, h=h: v.tensor_scalar_mul(
                    ex2_[h], pb[2 * h + 1][0:1, :], INV_D), sVE)
                wt("vector", sVE, v2)
                v3 = em("vector", lambda v, h=h: v.tensor_mul(var_[h], m_[h], m_[h]), sVE)
                wt("vector", sVE, v3)
                v4 = em("vector", lambda v, h=h: v.tensor_sub(var_[h], ex2_[h], var_[h]), sVE)
                wt("scalar", sVE, v4)
                s1_ = em("scalar", lambda s, h=h: s.activation(
                    rst_[h], var_[h], AF.Sqrt, bias=eps_ap[:, :]), sSE)
                wt("vector", sSE, s1_)
                v5 = em("vector", lambda v, h=h: v.reciprocal(rst_[h], rst_[h]), sVE)
                wt("vector", sVE, v5)
                em("vector", lambda v, h=h: v.tensor_mul(mr_[h], m_[h], rst_[h]), sVE)
            LN1_ROWS_VE = sVE.n
            wt("tensor", sVE, LN1_ROWS_VE)
            for h in range(2):
                em("tensor", lambda t, h=h: t.matmul(
                    pb[2 * h][:, :], ones_row[:, :], rst_[h], start=True, stop=True), None)
                em("tensor", lambda t, h=h: t.matmul(
                    pb[2 * h + 1][:, :], ones_row[:, :], mr_[h], start=True, stop=True),
                    sPE if h == 1 else None)
            LN1_BC_PE = sPE.n

            xf = [arena.view(o, [128, 1024], F32) for o in o_xf]
            xlo = [arena.view(o, [128, 512], BF16) for o in o_xlo]
            tmpv = arena.view(o_tmp, [128, 512], F32)
            wt("vector", sPE, LN1_BC_PE)
            last_xlo = [0, 0]
            last_v2 = 0
            for c in range(32):
                b = c % 2
                v = em("sync", lambda sy, c=c, b=b: sy.dma_start(
                    out=xf[b][:, :], in_=x_fm[c * 128:(c + 1) * 128, :]), sXF[b], 16)
                wt("vector", sXF[b], v)
                for h in range(2):
                    wt("vector", sVE, last_v2)
                    v1 = em("vector", lambda v_, b=b, h=h: v_.tensor_tensor(
                        tmpv[:, :], xf[b][:, h * 512:(h + 1) * 512],
                        pb[2 * h][:, :], ALU.mult), sVE)
                    wt("vector", sVE, v1)
                    wt("vector", sXLO[h], last_xlo[h])
                    v2 = em("vector", lambda v_, h=h: v_.tensor_tensor(
                        xlo[h][:, :], tmpv[:, :], pb[2 * h + 1][:, :], ALU.subtract), sVE)
                    last_v2 = v2
                    wt("sync", sVE, v2)
                    last_xlo[h] = em("sync", lambda sy, c=c, h=h: sy.dma_start(
                        out=xln_c[:, h * 16384 + c * 512: h * 16384 + (c + 1) * 512],
                        in_=xlo[h][:, :]), sXLO[h], 16)
            LN1_XLO = [sems["sXLO0"].n, sems["sXLO1"].n]
            LN1_VE = sVE.n

            pk1 = arena.view(o_pk, [128, 32, 64], I32)
            lsb1 = arena.view(o_lsb, [128, 2048], I32)
            scl1 = arena.view(o_scl, [128, 64], F32)
            dqf1 = arena.view(o_dqf, [128, 32, 2, 64], F32)
            dqb1 = arena.view(o_dqb, [128, 4096], BF16)
            stg1 = arena.view(o_stg, [128, 32, 128], BF16)
            pk2 = arena.view(o_pk, [128, 43, 64], I32)
            lsb2 = arena.view(o_lsb, [128, 2752], I32)
            scl2 = arena.view(o_scl, [128, 86], F32)
            dqf2 = arena.view(o_dqf, [128, 43, 2, 64], F32)
            dqb2 = arena.view(o_dqb, [128, 5504], BF16)
            stg2 = arena.view(o_stg, [128, 44, 128], BF16)

            prev = [0, 0, 0]

            def dequant_tile(kind, ti, wsrc, ssrc, cdst3):
                if kind in ("w1", "w3"):
                    ht = ti
                    hP = _hu(ht)
                    pk, lsbv, sclv, dqf, dqb, stg = pk1, lsb1, scl1, dqf1, dqb1, stg1
                    npk = 32 * hP
                    nfree = 2048
                    g_pk = lambda g: g.dma_start(
                        out=pk[0:hP, :, :],
                        in_=wsrc[4096 * ht: 4096 * ht + 32 * hP, :].rearrange(
                            "(p r) g -> p r g", r=32))
                    g_sc = lambda g: g.dma_start(
                        out=sclv[0:hP, :],
                        in_=ssrc[64 * 128 * ht: 64 * (128 * ht + hP)].rearrange(
                            "(p g) -> p g", g=64))
                    nP = hP
                    blocks = [(c * 128, 128, c) for c in range(32)]
                else:
                    dtile, half = ti
                    hP = 128
                    pk, lsbv, sclv, dqf, dqb, stg = pk2, lsb2, scl2, dqf2, dqb2, stg2
                    g_pk = lambda g: g.dma_start(
                        out=pk[:, :, :],
                        in_=wsrc[86 * 128 * dtile: 86 * 128 * (dtile + 1), :].rearrange(
                            "(p r) g -> p r g", r=86)[:, 43 * half: 43 * (half + 1), :])
                    g_sc = lambda g: g.dma_start(
                        out=sclv[:, :],
                        in_=s2_sl[172 * 128 * dtile: 172 * 128 * (dtile + 1)].rearrange(
                            "(p g) -> p g", g=172)[:, 86 * half: 86 * (half + 1)])
                    nP = 128
                    blocks = []
                    for u in range(44 * half, 44 * (half + 1)):
                        rtk, htk = u // NHT, u % NHT
                        start = HSL * rtk + 128 * htk - 5504 * half
                        blocks.append((start, _hu(htk), u - 44 * half))

                wt("scalar", sGE, CONST_GE)
                d1 = gdma(g_pk, extra_wait=(sVE, prev[0]) if prev[0] else None)
                d2 = gdma(g_sc)
                wt("vector", sGD, d2)
                wt("vector", sSE, prev[1])
                pkf = pk[0:nP, :, :].rearrange("p r g -> p (r g)")
                v1 = em("vector", lambda v: v.tensor_single_scalar(
                    lsbv[0:nP, :], pkf, 15, ALU.bitwise_and), sVE)
                wt("vector", sVE, v1)
                v2 = em("vector", lambda v: v.tensor_single_scalar(
                    lsbv[0:nP, :], lsbv[0:nP, :], 8, ALU.bitwise_xor), sVE)
                wt("vector", sVE, v2)
                v3 = em("vector", lambda v: v.tensor_single_scalar(
                    lsbv[0:nP, :], lsbv[0:nP, :], 8, ALU.subtract), sVE)
                wt("vector", sVE, v3)
                v4 = em("vector", lambda v: v.tensor_single_scalar(
                    pkf, pkf, 4, ALU.arith_shift_right), sVE)
                wt("vector", sVE, v4)
                v5 = em("vector", lambda v: v.tensor_copy(
                    dqf[0:nP, :, 0, :], pk[0:nP, :, :]), sVE)
                v6 = em("vector", lambda v: v.tensor_copy(
                    dqf[0:nP, :, 1, :],
                    lsbv[0:nP, :].rearrange("p (r g) -> p r g", g=64)), sVE)
                wt("vector", sVE, v6)
                nseg = dqf.shape[1]
                scb = sclv[0:nP, :].rearrange(
                    "p (r two) -> p r two", two=2).unsqueeze(3).to_broadcast(
                    [nP, nseg, 2, 64])
                v7 = em("vector", lambda v: v.tensor_tensor(
                    dqb[0:nP, :].rearrange("p (r two g) -> p r two g", two=2, g=64),
                    dqf[0:nP, :, :, :], scb, ALU.mult), sVE)
                prev[0] = v7
                wt("tensor", sVE, v7)
                wt("scalar", sDQ, prev[2])
                tr_marks = []
                for (cst, cl, cidx) in blocks:
                    bank = 6 + (cidx % 2)
                    if len(tr_marks) >= 2:
                        wt("tensor", sSE, tr_marks[-2])
                    em("tensor", lambda t, cst=cst, cl=cl, bank=bank, nP=nP: t.matmul(
                        pb[bank][0:cl, 0:nP], dqb[0:nP, cst:cst + cl],
                        ident[0:nP, 0:nP], start=True, stop=True), sPE)
                    pev = sPE.n
                    wt("scalar", sPE, pev)
                    se_ = em("scalar", lambda s, cidx=cidx, cl=cl, bank=bank, nP=nP, stg=stg: s.copy(
                        stg[0:cl, cidx, 0:nP], pb[bank][0:cl, 0:nP]), sSE)
                    tr_marks.append(se_)
                prev[1] = tr_marks[-1]
                wt("sync", sSE, prev[1])
                nst = stg.shape[1] if kind != "w2" else 44
                prev[2] = em("sync", lambda sy: sy.dma_start(
                    out=cdst3, in_=(stg1[:, :, :] if kind in ("w1", "w3")
                                    else stg2[:, 0:44, :])), sDQ, 16)

            w2t_c3 = w2t_c[:, :].rearrange("p (u d) -> p u d", d=W2D)
            for ht in range(NHT):
                dequant_tile("w1", ht, w1_sl, s1_sl,
                             w1t_c[:, ht * D:(ht + 1) * D].rearrange(
                                 "p (c j) -> p c j", j=128))
            for ht in range(NHT):
                dequant_tile("w3", ht, w3_sl, s3_sl,
                             w3t_c[:, ht * D:(ht + 1) * D].rearrange(
                                 "p (c j) -> p c j", j=128))
            for dtile in range(4):
                for half in range(2):
                    cdst3 = w2t_c3[:, 44 * half:44 * (half + 1),
                                   dtile * 128:(dtile + 1) * 128]
                    dequant_tile("w2", (dtile, half), w2_sl, s2_sl, cdst3)
            DEQ_DQ, DEQ_VE, DEQ_SE, DEQ_PE, DEQ_GD = sDQ.n, sVE.n, sSE.n, sPE.n, sGD.n

            wt("gpsimd", sXLO[0], LN1_XLO[0])
            wt("gpsimd", sXLO[1], LN1_XLO[1])
            em("gpsimd", lambda g: g.collective_compute(
                "AllGather", ALU.bypass, replica_groups=[list(range(NC))],
                ins=[xln_c[:, :]], outs=[xln_g[:, :]]), sCC)
            CC_XLN = sCC.n

            wt("gpsimd", sDQ, DEQ_DQ)
            em("gpsimd", lambda g: g.collective_compute(
                "AllGather", ALU.bypass, replica_groups=[list(range(NC))],
                ins=[w1t_c[:, :]], outs=[w1t_g[:, :]]), sCC)
            em("gpsimd", lambda g: g.collective_compute(
                "AllGather", ALU.bypass, replica_groups=[list(range(NC))],
                ins=[w3t_c[:, :]], outs=[w3t_g[:, :]]), sCC)
            em("gpsimd", lambda g: g.collective_compute(
                "AllGather", ALU.bypass, replica_groups=[list(range(NC))],
                ins=[w2t_c[:, :]], outs=[w2t_g[:, :]]), sCC)
            CC_W = sCC.n

            fence_all()
            if upto <= 1:
                dummy_out()
                return

            xl = [arena.view(o, [128, 32, 512], BF16) for o in o_xl]
            evv = arena.view(o_ev, [128, 4, 512], BF16)
            ev_free = [0] * 4
            qb_pe = [0] * 16
            for qb in range(16):
                b = qb % 2
                rsrc, qh = qb // 2, qb % 2
                if qb >= 2:
                    wt("sync", sPE, qb_pe[qb - 2])
                xld = em("sync", lambda sy, b=b, rsrc=rsrc, qh=qh: sy.dma_start(
                    out=xl[b][:, :, :],
                    in_=xln_g[128 * rsrc:128 * (rsrc + 1),
                              qh * 16384:(qh + 1) * 16384].rearrange(
                        "p (c q) -> p c q", q=512)), sXL[b], 16)
                wt("tensor", sXL[b], xld)
                for wi, (w_v, tmaj) in enumerate(((wq_v, False), (wk_v, False),
                                                  (wv_v, True))):
                    for c in range(32):
                        for hj in range(4):
                            if not tmaj:
                                em("tensor", lambda t, w_v=w_v, c=c, hj=hj, b=b: t.matmul(
                                    pb[hj][:, :], w_v[:, c, hj * 128:(hj + 1) * 128],
                                    xl[b][:, c, :], start=(c == 0), stop=(c == 31)),
                                    sPE if (c == 31 and hj == 3) else None)
                            else:
                                em("tensor", lambda t, w_v=w_v, c=c, hj=hj, b=b: t.matmul(
                                    pb[hj][:, :], xl[b][:, c, hj * 128:(hj + 1) * 128],
                                    w_v[:, c, :], start=(c == 0), stop=(c == 31)),
                                    sPE if (c == 31 and hj == 3) else None)
                    pev = sPE.n
                    for hj in range(4):
                        wt("scalar", sPE, pev)
                        wt("scalar", sEV[hj], ev_free[hj])
                        sev = em("scalar", lambda s, hj=hj: s.copy(
                            evv[:, hj, :], pb[hj][:, :]), sSE)
                        wt("sync", sSE, sev)
                        if wi == 0:
                            ev_free[hj] = em("sync", lambda sy, hj=hj, qb=qb: sy.dma_start(
                                out=q_dram[hj * 128:(hj + 1) * 128, qb * 512:(qb + 1) * 512],
                                in_=evv[:, hj, :]), sEV[hj], 16)
                        elif wi == 1:
                            ev_free[hj] = em("sync", lambda sy, hj=hj, qb=qb: sy.dma_start(
                                out=k_dram[hj * 128:(hj + 1) * 128, qb * 512:(qb + 1) * 512],
                                in_=evv[:, hj, :]), sEV[hj], 16)
                        else:
                            ev_free[hj] = em("sync", lambda sy, hj=hj, qb=qb: sy.dma_start(
                                out=v_dram[qb * 512 + hj * 128: qb * 512 + (hj + 1) * 128, :],
                                in_=evv[:, hj, :]), sEV[hj], 16)
                    wt("tensor", sSE, sSE.n)
                qb_pe[qb] = sPE.n

            fence_all()
            if upto <= 2:
                dummy_out()
                return

            Kv = arena.view(o_K, [128, 4, 8192], BF16)
            Vv = arena.view(o_V, [128, 64, 512], BF16)
            Qv = [arena.view(o, [128, 512], BF16) for o in o_Q]
            Pv = [arena.view(o, [128, 512], BF16) for o in o_pr]
            Lb = arena.view(o_li, [128, 512], F32)
            Ao = [arena.view(o, [128, 512], BF16) for o in o_ao]
            lrow = rows[0:1, 4096:4608]
            kld = em("sync", lambda sy: sy.dma_start(
                out=Kv[:, :, :],
                in_=k_dram[:, :].rearrange("(hj p) k -> p hj k", p=128)), sKV, 16)
            vld = em("sync", lambda sy: sy.dma_start(
                out=Vv[:, :, :],
                in_=v_dram[:, :].rearrange("(tt p) e -> p tt e", p=128)), sKV, 16)
            sc_done = [0, 0, 0]
            acc_done = [0, 0]
            ao_dma = [0, 0]
            q_dma = [0, 0]
            pe_hist = []
            it = 0
            SD = 3
            sbank = [0, 1, 7]
            for hj in range(4):
                for qb in range(16):
                    ab = it % 2
                    qbuf = it % 2
                    if it >= 2:
                        wt("sync", sPE, pe_hist[it - 2])
                    q_dma[qbuf] = em("sync", lambda sy, hj=hj, qb=qb, qbuf=qbuf: sy.dma_start(
                        out=Qv[qbuf][:, :],
                        in_=q_dram[hj * 128:(hj + 1) * 128, qb * 512:(qb + 1) * 512]),
                        sQ[qbuf], 16)
                    wt("tensor", sQ[qbuf], q_dma[qbuf])
                    if it < 2:
                        wt("tensor", sKV, vld)
                    wt("tensor", sVE, acc_done[ab])
                    nkt = 4 * qb + 4

                    def emit_scores(kt, hj=hj, qb=qb, qbuf=qbuf):
                        sb = kt % SD
                        wt("tensor", sSE, sc_done[sb])
                        pe1 = em("tensor", lambda t, hj=hj, kt=kt, sb=sb, qbuf=qbuf: t.matmul(
                            pb[sbank[sb]][:, :], Kv[:, hj, kt * 128:(kt + 1) * 128],
                            Qv[qbuf][:, :], start=True, stop=True), sPE)
                        if kt >= 4 * qb:
                            mm = kt - 4 * qb
                            wt("vector", sPE, pe1)
                            mdone = em("vector", lambda v, sb=sb, mm=mm: v.tensor_tensor(
                                pb[sbank[sb]][:, :], pb[sbank[sb]][:, :],
                                mask_big[:, 384 - 128 * mm: 896 - 128 * mm], ALU.add), sVE)
                            wt("scalar", sVE, mdone)
                        else:
                            wt("scalar", sPE, pe1)
                        sc_done[sb] = em("scalar", lambda s, sb=sb: s.activation(
                            Pv[sb][:, :], pb[sbank[sb]][:, :], AF.Exp,
                            bias=sb_b[:, :], scale=sc_c[:, :]), sSE)

                    def emit_pv(kt, nkt, hj=hj, ab=ab):
                        sb = kt % SD
                        wt("tensor", sSE, sc_done[sb])
                        em("tensor", lambda t, hj=hj, kt=kt, sb=sb, ab=ab: t.matmul(
                            pb[2 + ab][:, :], Vv[:, kt, hj * 128:(hj + 1) * 128],
                            Pv[sb][:, :], start=(kt == 0), stop=(kt == nkt - 1)), None)
                        em("tensor", lambda t, kt=kt, sb=sb, ab=ab, nkt=nkt: t.matmul(
                            pb[4 + ab][0:1, :], ones_col[:, :], Pv[sb][:, :],
                            start=(kt == 0), stop=(kt == nkt - 1)),
                            sPE if kt == nkt - 1 else None)

                    for kt in range(nkt + SD - 1):
                        if kt < nkt:
                            emit_scores(kt)
                        kc = kt - (SD - 1)
                        if kc >= 0:
                            emit_pv(kc, nkt)
                    pe_end = sPE.n
                    wt("vector", sPE, pe_end)
                    v1 = em("vector", lambda v, ab=ab: v.reciprocal(
                        lrow, pb[4 + ab][0:1, :]), sVE)
                    wt("tensor", sVE, v1)
                    p_bc = em("tensor", lambda t: t.matmul(
                        pb[6][:, :], ones_row[:, :], lrow, start=True, stop=True), sPE)
                    pe_hist.append(sPE.n)
                    wt("vector", sPE, p_bc)
                    wt("vector", sVE, acc_done[(it - 1) % 2] if it >= 1 else 0)
                    v2 = em("vector", lambda v: v.tensor_copy(Lb[:, :], pb[6][:, :]), sVE)
                    wt("vector", sVE, v2)
                    wt("vector", sATs[qbuf], ao_dma[qbuf])
                    acc_done[ab] = em("vector", lambda v, ab=ab, qbuf=qbuf: v.tensor_tensor(
                        Ao[qbuf][:, :], pb[2 + ab][:, :], Lb[:, :], ALU.mult), sVE)
                    wt("sync", sVE, acc_done[ab])
                    ao_dma[qbuf] = em("sync", lambda sy, hj=hj, qb=qb, qbuf=qbuf: sy.dma_start(
                        out=attn_c[hj * 128:(hj + 1) * 128, qb * 512:(qb + 1) * 512],
                        in_=Ao[qbuf][:, :]), sATs[qbuf], 16)
                    it += 1

            wt("gpsimd", sATs[0], sems["sAT0"].n)
            wt("gpsimd", sATs[1], sems["sAT1"].n)
            em("gpsimd", lambda g: g.collective_compute(
                "AllGather", ALU.bypass, replica_groups=[list(range(NC))],
                ins=[attn_c[:, :]], outs=[attn_g[:, :]]), sCC)
            CC_AT = sCC.n

            fence_all()
            if upto <= 3:
                dummy_out()
                return

            arv = arena.view(o_ar, [128, 32, 512], BF16)
            wov = [arena.view(o, [128, 32, 256], BF16) for o in o_wo]
            x1s = arena.view(o_x1s, [128, 32, 512], F32)
            xtv = [arena.view(o, [128, 512], F32) for o in o_xt]
            x1b = arena.view(o_b1, [128, 512], BF16)
            xqb = arena.view(o_b2, [128, 512], BF16)
            t2v = arena.view(o_t2, [128, 512], F32)
            xl2o = [arena.view(o, [128, 512], BF16) for o in o_xl2o]
            ag3 = attn_g[:, :].rearrange("(ec p) q -> p ec q", p=128)
            m2 = rows[0:1, 0:512]
            e2 = rows[0:1, 512:1024]
            va2 = rows[0:1, 1024:1536]
            rs2 = rows[0:1, 1536:2048]
            mr2 = rows[0:1, 2048:2560]
            wo_pe_h = [0, 0, 0]
            last_wo_pe = 0
            xt_ve = [0, 0]
            x1s_mark = 0
            xl2_st = [0, 0]
            for qt in range(2):
                wt("sync", sPE, last_wo_pe)
                wt("vector", sX1S, x1s_mark)
                ar_d = em("sync", lambda sy, qt=qt: sy.dma_start(
                    out=arv[:, :, :],
                    in_=ag3[:, :, bass.ds((E_.rank * T + qt * 512) if E_.name == "sync"
                                          else qt * 512, 512)]), sAR, 16)
                st_pe = []
                for dtp in range(16):
                    wb = dtp % 3
                    wt("sync", sPE, wo_pe_h[wb])
                    wo_d = em("sync", lambda sy, dtp=dtp, wb=wb: sy.dma_start(
                        out=wov[wb][:, :, :],
                        in_=wo_bf[:, dtp * 256:(dtp + 1) * 256].rearrange(
                            "(ec p) d -> p ec d", p=128)), sWO[wb], 16)
                    wt("tensor", sWO[wb], wo_d)
                    wt("tensor", sAR, ar_d)
                    for dh in range(2):
                        dt = dtp * 2 + dh
                        bank = dt % 2
                        wt("tensor", sVE, xt_ve[bank])
                        for ec in range(32):
                            em("tensor", lambda t, wb=wb, dh=dh, ec=ec, bank=bank: t.matmul(
                                pb[bank][:, :], wov[wb][:, ec, dh * 128:(dh + 1) * 128],
                                arv[:, ec, :], start=(ec == 0), stop=(ec == 31)),
                                sPE if ec == 31 else None)
                        pev = sPE.n
                        xb_ = dt % 2
                        wt("sync", sVE, xt_ve[xb_])
                        xd = em("sync", lambda sy, dt=dt, qt=qt, xb_=xb_: sy.dma_start(
                            out=xtv[xb_][:, :],
                            in_=x_fm[dt * 128:(dt + 1) * 128, qt * 512:(qt + 1) * 512]),
                            sXT[xb_], 16)
                        wt("vector", sPE, pev)
                        wt("vector", sXT[xb_], xd)
                        xt_ve[xb_] = em("vector", lambda v, dt=dt, bank=bank, xb_=xb_: v.tensor_tensor(
                            x1s[:, dt, :], pb[bank][:, :], xtv[xb_][:, :], ALU.add), sVE)
                        v1 = xt_ve[xb_]
                        wt("scalar", sVE, v1)
                        if dt >= 1:
                            wt("scalar", sPE, st_pe[dt - 1])
                        s1_ = em("scalar", lambda s, dt=dt: s.copy(
                            x1b[:, :], x1s[:, dt, :]), sSE)
                        wt("scalar", sSE, s1_)
                        s2_ = em("scalar", lambda s, dt=dt: s.activation(
                            xqb[:, :], x1s[:, dt, :], AF.Square), sSE)
                        wt("tensor", sSE, s2_)
                        em("tensor", lambda t, dt=dt: t.matmul(
                            pb[2][0:1, :], ones_col[:, :], x1b[:, :],
                            start=(dt == 0), stop=(dt == 31)), None)
                        pe2 = em("tensor", lambda t, dt=dt: t.matmul(
                            pb[3][0:1, :], ones_col[:, :], xqb[:, :],
                            start=(dt == 0), stop=(dt == 31)), sPE)
                        st_pe.append(pe2)
                        wt("sync", sVE, v1)
                        em("sync", lambda sy, dt=dt, qt=qt: sy.dma_start(
                            out=x1_dram[dt * 128:(dt + 1) * 128, qt * 512:(qt + 1) * 512],
                            in_=x1s[:, dt, :]), sX1S, 16)
                    wo_pe_h[wb] = sPE.n
                last_wo_pe = sPE.n
                x1s_mark = sX1S.n
                wt("vector", sPE, st_pe[-1])
                em("vector", lambda v: v.tensor_scalar_mul(m2, pb[2][0:1, :], INV_D), sVE)
                v2_ = em("vector", lambda v: v.tensor_scalar_mul(e2, pb[3][0:1, :], INV_D), sVE)
                wt("vector", sVE, v2_)
                v3_ = em("vector", lambda v: v.tensor_mul(va2, m2, m2), sVE)
                wt("vector", sVE, v3_)
                v4_ = em("vector", lambda v: v.tensor_sub(va2, e2, va2), sVE)
                wt("scalar", sVE, v4_)
                s3_ = em("scalar", lambda s: s.activation(
                    rs2, va2, AF.Sqrt, bias=eps_ap[:, :]), sSE)
                wt("vector", sSE, s3_)
                v5_ = em("vector", lambda v: v.reciprocal(rs2, rs2), sVE)
                wt("vector", sVE, v5_)
                v6_ = em("vector", lambda v: v.tensor_mul(mr2, m2, rs2), sVE)
                wt("tensor", sVE, v6_)
                em("tensor", lambda t: t.matmul(
                    pb[4][:, :], ones_row[:, :], rs2, start=True, stop=True), None)
                bc2 = em("tensor", lambda t: t.matmul(
                    pb[5][:, :], ones_row[:, :], mr2, start=True, stop=True), sPE)
                wt("vector", sPE, bc2)
                last_v8 = 0
                for dt in range(32):
                    ob = dt % 2
                    wt("vector", sVE, last_v8)
                    v7 = em("vector", lambda v, dt=dt: v.tensor_tensor(
                        t2v[:, :], x1s[:, dt, :], pb[4][:, :], ALU.mult), sVE)
                    wt("vector", sVE, v7)
                    wt("vector", sXO[ob], xl2_st[ob])
                    v8 = em("vector", lambda v, ob=ob: v.tensor_tensor(
                        xl2o[ob][:, :], t2v[:, :], pb[5][:, :], ALU.subtract), sVE)
                    last_v8 = v8
                    wt("sync", sVE, v8)
                    xl2_st[ob] = em("sync", lambda sy, dt=dt, qt=qt, ob=ob: sy.dma_start(
                        out=xl2_dram[:, qt * 16384 + dt * 512: qt * 16384 + (dt + 1) * 512],
                        in_=xl2o[ob][:, :]), sXO[ob], 16)

            fence_all()
            if upto <= 4:
                dummy_out()
                return

            hact = arena.view(o_hact, [128, NSLOT, 512], BF16)
            xl2v = arena.view(o_xl2, [128, 32, 512], BF16)
            w13 = [arena.view(o, [128, 32, 128], BF16) for o in o_w13]
            htmp = arena.view(o_ht, [128, 512], F32)
            w2b = [arena.view(o, [128, 22, 512], BF16) for o in o_w2b]
            x1t = [arena.view(o, [128, 512], F32) for o in o_x1t]
            osb = [arena.view(o, [128, 512], F32) for o in o_os]
            w2g3 = [w2t_g[128 * rd:128 * (rd + 1), :].rearrange(
                "p (u d) -> p u d", d=512) for rd in range(NC)]
            out_st = [0, 0]
            ab_pe_h = [0, 0, 0]
            hact_ve_h = [0, 0]
            w2_pe_h = [0, 0, 0]
            prev_qt = {"pe": 0, "ve": 0, "st0": 0, "st1": 0}
            for qt in range(2):
                for eng in ("sync", "scalar"):
                    wt(eng, sPE, prev_qt["pe"])
                    wt(eng, sVE, prev_qt["ve"])
                    wt(eng, sSTs[0], prev_qt["st0"])
                    wt(eng, sSTs[1], prev_qt["st1"])
                x2d = em("sync", lambda sy, qt=qt: sy.dma_start(
                    out=xl2v[:, :, :],
                    in_=xl2_dram[:, qt * 16384:(qt + 1) * 16384].rearrange(
                        "p (c q) -> p c q", q=512)), sX2, 16)
                for u in range(NSLOT):
                    ru, htu = u // NHT, u % NHT
                    hu = _hu(htu)
                    b1_, b3_ = (u % 2) * 2, (u % 2) * 2 + 1
                    wbuf1, wbuf3 = (u % 3), 3 + (u % 3)
                    wt("sync", sPE, ab_pe_h[u % 3])
                    w1d = em("sync", lambda sy, ru=ru, htu=htu, wbuf1=wbuf1: sy.dma_start(
                        out=w13[wbuf1][:, :, :],
                        in_=w1t_g[128 * ru:128 * (ru + 1),
                                  htu * D:(htu + 1) * D].rearrange(
                            "p (c j) -> p c j", j=128)), sW13[wbuf1], 16)
                    wd = em("sync", lambda sy, ru=ru, htu=htu, wbuf3=wbuf3: sy.dma_start(
                        out=w13[wbuf3][:, :, :],
                        in_=w3t_g[128 * ru:128 * (ru + 1),
                                  htu * D:(htu + 1) * D].rearrange(
                            "p (c j) -> p c j", j=128)), sW13[wbuf3], 16)
                    wt("tensor", sW13[wbuf1], w1d)
                    wt("tensor", sW13[wbuf3], wd)
                    wt("tensor", sX2, x2d)
                    wt("tensor", sVE, hact_ve_h[u % 2])
                    for c in range(32):
                        em("tensor", lambda t, c=c, hu=hu, b1_=b1_, wbuf1=wbuf1: t.matmul(
                            pb[b1_][0:hu, :], w13[wbuf1][:, c, 0:hu], xl2v[:, c, :],
                            start=(c == 0), stop=(c == 31)), None)
                        em("tensor", lambda t, c=c, hu=hu, b3_=b3_, wbuf3=wbuf3: t.matmul(
                            pb[b3_][0:hu, :], w13[wbuf3][:, c, 0:hu], xl2v[:, c, :],
                            start=(c == 0), stop=(c == 31)),
                            sPE if c == 31 else None)
                    ab_pe_h[u % 3] = sPE.n
                    wt("scalar", sPE, ab_pe_h[u % 3])
                    if u >= 1:
                        wt("scalar", sVE, hact_ve_h[(u - 1) % 2])
                    sl = em("scalar", lambda s, hu=hu, b1_=b1_: s.activation(
                        htmp[0:hu, :], pb[b1_][0:hu, :], AF.Silu), sSE)
                    wt("vector", sSE, sl)
                    hact_ve_h[u % 2] = em("vector", lambda v, u=u, hu=hu, b3_=b3_: v.tensor_tensor(
                        hact[0:hu, u, :], pb[b3_][0:hu, :], htmp[0:hu, :], ALU.mult), sVE)
                AB_VE = sVE.n
                AB_PE = sPE.n
                wt("tensor", sVE, AB_VE)
                wt("scalar", sPE, AB_PE)
                wt("scalar", sVE, AB_VE)
                wt("sync", sPE, AB_PE)
                wt("sync", sVE, AB_VE)
                w2_evac_ve = 0
                for rd in range(NC):
                    wt("tensor", sVE, w2_evac_ve)
                    for usub in range(4):
                        wg = rd * 4 + usub
                        wb = wg % 3
                        wt("sync", sPE, w2_pe_h[wb])
                        w2_d = em("sync", lambda sy, rd=rd, usub=usub, wb=wb: sy.dma_start(
                            out=w2b[wb][:, :, :],
                            in_=w2g3[rd][:, 22 * usub:22 * (usub + 1), :]), sW2[wb], 16)
                        wt("tensor", sW2[wb], w2_d)
                        for ui in range(22):
                            u = usub * 22 + ui
                            hu = _hu(u % NHT)
                            for dj in range(4):
                                em("tensor", lambda t, wb=wb, ui=ui, dj=dj, u=u, hu=hu: t.matmul(
                                    pb[4 + dj][:, :],
                                    w2b[wb][0:hu, ui, dj * 128:(dj + 1) * 128],
                                    hact[0:hu, u, :],
                                    start=(u == 0), stop=(u == NSLOT - 1)),
                                    sPE if (ui == 21 and dj == 3) else None)
                        w2_pe_h[wb] = sPE.n
                    pev = sPE.n
                    for dj in range(4):
                        dt = rd * 4 + dj
                        xb_ = dj % 2
                        xd = em("sync", lambda sy, dt=dt, qt=qt, xb_=xb_: sy.dma_start(
                            out=x1t[xb_][:, :],
                            in_=x1_dram[dt * 128:(dt + 1) * 128,
                                        qt * 512:(qt + 1) * 512]), sX1T[xb_], 16)
                        ob = dj % 2
                        wt("vector", sPE, pev)
                        wt("vector", sX1T[xb_], xd)
                        wt("vector", sSTs[ob], out_st[ob])
                        vv = em("vector", lambda v, dj=dj, xb_=xb_, ob=ob: v.tensor_tensor(
                            osb[ob][:, :], pb[4 + dj][:, :], x1t[xb_][:, :], ALU.add), sVE)
                        wt("sync", sVE, vv)
                        out_st[ob] = em("sync", lambda sy, dt=dt, qt=qt, ob=ob: sy.dma_start(
                            out=out_fm[dt * 128:(dt + 1) * 128, qt * 512:(qt + 1) * 512],
                            in_=osb[ob][:, :]), sSTs[ob], 16)
                    w2_evac_ve = sVE.n
                prev_qt = {"pe": sPE.n, "ve": sVE.n,
                           "st0": sSTs[0].n, "st1": sSTs[1].n}

            wt("sync", sSTs[0], sems["sST0"].n)
            wt("sync", sSTs[1], sems["sST1"].n)
            wt("gpsimd", sSTs[0], sems["sST0"].n)
            wt("gpsimd", sSTs[1], sems["sST1"].n)

        for name, setter in (("gpsimd", blk.gpsimd), ("scalar", blk.scalar),
                             ("tensor", blk.tensor), ("vector", blk.vector),
                             ("sync", blk.sync)):
            def body(obj, name=name):
                program(E(name, obj))
            setter(body)

    return nc


_NC_CACHE = None


def _get_nc():
    global _NC_CACHE
    if _NC_CACHE is None:
        _NC_CACHE = build_nc()
    return _NC_CACHE


def make_in_maps(inputs):
    x = np.asarray(inputs["L0_attn_hidden_states"])
    wq = np.asarray(inputs["L0_attn_wq"])
    wkv = np.asarray(inputs["L0_attn_wkv"])
    wo = np.asarray(inputs["L0_attn_wo"])
    w1 = np.asarray(inputs["L0_ffn_w1"])
    s1 = np.asarray(inputs["L0_ffn_s1"])
    w2 = np.asarray(inputs["L0_ffn_w2"])
    s2 = np.asarray(inputs["L0_ffn_s2"])
    w3 = np.asarray(inputs["L0_ffn_w3"])
    s3 = np.asarray(inputs["L0_ffn_s3"])

    perm = [r + 8 * j for r in range(8) for j in range(4)]
    wo_p = np.ascontiguousarray(wo.reshape(32, 128, D)[perm].reshape(D, D))

    in_maps = []
    for r in range(NC):
        heads = [r + 8 * j for j in range(HJ)]
        wq_sl = np.ascontiguousarray(
            np.concatenate([wq[:, h * 192:h * 192 + 128] for h in heads], axis=1))
        wk_sl = np.ascontiguousarray(
            np.concatenate([wkv[:, h * 128:h * 128 + 128] for h in heads], axis=1))
        wv_sl = np.ascontiguousarray(
            np.concatenate([wkv[:, D + h * 128:D + h * 128 + 128] for h in heads],
                           axis=1))
        x_fm = np.ascontiguousarray(x[0, r * T:(r + 1) * T, :].T)
        in_maps.append({
            "x_fm": x_fm, "wq_sl": wq_sl, "wk_sl": wk_sl, "wv_sl": wv_sl,
            "wo_p": wo_p,
            "w1_sl": np.ascontiguousarray(w1[32 * HSL * r: 32 * HSL * (r + 1)]),
            "s1_sl": np.ascontiguousarray(s1[64 * HSL * r: 64 * HSL * (r + 1)]),
            "w3_sl": np.ascontiguousarray(w3[32 * HSL * r: 32 * HSL * (r + 1)]),
            "s3_sl": np.ascontiguousarray(s3[64 * HSL * r: 64 * HSL * (r + 1)]),
            "w2_sl": np.ascontiguousarray(w2[86 * W2D * r: 86 * W2D * (r + 1)]),
            "s2_sl": np.ascontiguousarray(s2[172 * W2D * r: 172 * W2D * (r + 1)]),
        })
    return in_maps


def kernel(**inputs):
    in_maps = make_in_maps(inputs)
    nc = _get_nc()
    res = run_bass_kernel_spmd(nc, in_maps, core_ids=list(range(NC)))
    out = np.empty((S, D), np.float32)
    for r in range(NC):
        out[r * T:(r + 1) * T] = res.results[r]["out_fm"].T
    return out.reshape(1, S, D)
